# version 09
# baseline (speedup 1.0000x reference)
import numpy as np
import ml_dtypes
import concourse.bass as bass
import concourse.mybir as mybir
from concourse.bass_utils import run_bass_kernel_spmd

F32 = mybir.dt.float32
BF16 = mybir.dt.bfloat16
AF = mybir.ActivationFunctionType
ALU = mybir.AluOpType
AX = mybir.AxisListType
BF = ml_dtypes.bfloat16

D = 2048
S = 2048
NB = 4
HALF = 1024
NH = 16
DH = 128
NG = 4
GH = 4
DC = 16
VT = 136
VHS = 16 * VT
HFF = 1408
NHC = 11
NEP = 10
NDS = 4
SW_HCS = (3, 3, 3, 2)
INV_SQRT_DH = float(1.0 / np.sqrt(128.0))
EXP_BIAS = -12.0
LN_EPS = 1e-5

DEBUG_PHASE = 0


class _Null:
    def __getattr__(self, k):
        return lambda *a, **kw: None


_NULL = _Null()


class Ctx:

    def __init__(self, nc, sems):
        self.nc = nc
        self.sems = sems
        self.ev = {}
        self.waits = {}
        self.cur = None
        self.cnt = {}

    def begin(self, cur):
        self.cur = cur
        self.cnt = {k: 0 for k in self.sems}

    def eng(self, name):
        return getattr(self.nc, name) if self.cur == name else _NULL

    def inc(self, engname, sem, ins, ev=None, n=1):
        self.cnt[sem] += n
        if ev is not None:
            self.ev[ev] = self.cnt[sem]
        if self.cur == engname and ins is not None:
            ins.then_inc(self.sems[sem], n)
        return self.cnt[sem]

    def wait(self, engname, sem, ev):
        w = self.waits.setdefault(engname, [])
        if sem not in w:
            w.append(sem)
        if self.cur == engname:
            v = self.ev[ev] if isinstance(ev, str) else ev
            getattr(self.nc, engname).wait_ge(self.sems[sem], v)


def build_program():
    nc = bass.Bass("TRN2", target_bir_lowering=False,
                   detect_race_conditions=False)

    x_d = nc.dram_tensor("x", [S, D], F32, kind="ExternalInput").ap()
    wqkvt_d = nc.dram_tensor("wqkvt", [D, 3 * D], F32, kind="ExternalInput").ap()
    woutt_d = nc.dram_tensor("woutt", [D, D], F32, kind="ExternalInput").ap()
    gatew_d = nc.dram_tensor("gatew", [D, 8], F32, kind="ExternalInput").ap()
    wg_d = nc.dram_tensor("wg", [NEP, D * HFF], F32, kind="ExternalInput").ap()
    wu_d = nc.dram_tensor("wu", [NEP, D * HFF], F32, kind="ExternalInput").ap()
    wdn_d = nc.dram_tensor("wdn", [NEP, NDS, HFF, 512], F32,
                           kind="ExternalInput").ap()
    out_d = nc.dram_tensor("out", [HALF, D], F32, kind="ExternalOutput").ap()
    x1_d = nc.dram_tensor("x1b", [HALF, D], F32, kind="Internal").ap()
    dbg_d = None
    if DEBUG_PHASE:
        dbg_d = nc.dram_tensor("dbg", [128, 32768], F32,
                               kind="ExternalOutput").ap()

    x_t = x_d.rearrange("(n p) d -> n p d", p=128)
    x1_t = x1_d.rearrange("(n p) d -> n p d", p=128)
    out_t = out_d.rearrange("(n p) d -> n p d", p=128)

    sw_off = [0]
    for w in SW_HCS:
        sw_off.append(sw_off[-1] + DC * 128 * (w * 128))

    BASE = 16512
    LIMIT = 229300
    ptr = [BASE]
    smap = {}

    def place(name, shape, dtype, at=None):
        nbytes = int(np.prod(shape[1:])) * mybir.dt.size(dtype)
        if at is None:
            o = (ptr[0] + 31) // 32 * 32
            ptr[0] = o + nbytes
            assert ptr[0] <= LIMIT, (name, ptr[0])
        else:
            o = smap[at][0] if isinstance(at, str) else at
        h = nc.alloc_sbuf_tensor_at(name, shape, dtype, offset=o)
        smap[name] = (o, nbytes)
        return h

    ident = place("ident", [128, 128], BF16)
    smalls = place("smalls", [128, 128], F32)
    onesb = place("onesb", [128, 16], BF16)
    onesrow_b = place("onesrow_b", [1, 128], BF16)
    gatew_s = place("gatew_s", [128, DC * 8], BF16)
    combtm = place("combtm", [128, 64], F32)
    o_all = place("o_all", [128, NH * 8 * 128], BF16)
    ln2fm = place("ln2fm", [128, DC * HALF], BF16, at="o_all")
    ln1fm = place("ln1fm", [128, DC * S], BF16)
    attout = place("attout", [128, 8 * D], F32, at="ln1fm")
    ffnacc = place("ffnacc", [128, 8 * D], F32, at="ln1fm")
    x0ring = place("x0ring", [128, 2 * D], F32)
    lnscr = place("lnscr", [128, 2 * D], BF16)
    qg = place("qg", [128, GH * HALF], BF16)
    kg = place("kg", [128, GH * S], BF16)
    vg = place("vg", [128, GH * VHS], BF16)
    at_b = place("at_b", [128, 16 * 256], BF16)
    wring = place("wring", [128, 2 * DC * 512], BF16)
    vartrash = place("vartrash", [128, D], BF16)
    o_fm = place("o_fm", [128, NH * HALF], BF16, at="kg")
    wring2 = place("wring2", [128, 2 * DC * 512], BF16, at="wring")
    gbuf = place("gbuf", [128, NHC * HALF], BF16, at="qg")
    cptr = [smap["qg"][0] + NHC * HALF * 2]

    def placeC(name, shape, dtype):
        nbytes = int(np.prod(shape[1:])) * mybir.dt.size(dtype)
        o = (cptr[0] + 31) // 32 * 32
        cptr[0] = o + nbytes
        h = nc.alloc_sbuf_tensor_at(name, shape, dtype, offset=o)
        smap[name] = (o, nbytes)
        return h

    wgring = placeC("wgring", [128, 2 * DC * 384], BF16)
    wdnring = placeC("wdnring", [128, 2 * NHC * 512], BF16)
    assert cptr[0] <= smap["wring"][0] + smap["wring"][1], (
        "C scratch overflow", cptr[0])

    ps_s0 = nc.alloc_psum_tensor("ps_s0", [128, 1024], F32)
    ps_s1 = nc.alloc_psum_tensor("ps_s1", [128, 1024], F32)
    ps_2 = nc.alloc_psum_tensor("ps_2", [128, 512], F32)
    ps_3 = nc.alloc_psum_tensor("ps_3", [128, 512], F32)
    mark0 = nc.psum_base
    ps_d = nc.alloc_psum_tensor("ps_d", [128, 512], F32)
    mark = nc.psum_base
    ps_t = nc.alloc_psum_tensor("ps_t", [128, 1024], BF16)
    nc.psum_base = mark0
    ps_t2 = nc.alloc_psum_tensor("ps_t2", [128, 1024], BF16)
    nc.psum_base = mark
    ps_d2 = nc.alloc_psum_tensor("ps_d2", [128, 512], F32)
    assert nc.psum_base <= 8

    eps_ap = smalls[:, 0:1]
    ebias_ap = smalls[:, 1:2]
    onescol_b = onesb[:, 0:1]
    onesrow_f = smalls[0:1, 20:21]

    sem_names = ["sx", "sw", "pe", "ve", "ac", "gp", "st", "dn"]

    def program(ctx):
        sy = ctx.eng("sync")
        gp = ctx.eng("gpsimd")
        pe = ctx.eng("tensor")
        ve = ctx.eng("vector")
        ac = ctx.eng("scalar")

        gp.memset(ident[:], 0.0)
        gp.affine_select(out=ident[:], in_=ident[:], compare_op=ALU.not_equal,
                         fill=1.0, base=0, pattern=[[-1, 128]],
                         channel_multiplier=1)
        ins = gp.memset(onesrow_b[:], 1.0)
        ctx.inc("gpsimd", "gp", ins, ev="ident")
        ve.memset(smalls[:], 0.0)
        ve.memset(eps_ap, LN_EPS)
        ve.memset(ebias_ap, EXP_BIAS)
        ve.memset(onesrow_f, 1.0)
        for xv in range(GH * 16):
            ve.memset(vg[:, xv * VT + 128:xv * VT + 129], 1.0)
        ins = ve.memset(onesb[:], 1.0)
        ctx.inc("vector", "ve", ins, ev="init_ve")
        for i in range(2):
            eng, engn = (sy, "sync") if i % 2 == 0 else (ac, "scalar")
            ins = eng.dma_start(out=x0ring[:, (i % 2) * D:(i % 2) * D + D],
                                in_=x_t[i])
            ctx.inc(engn, "sx", ins, ev=f"x{i}", n=16)
        for dcc in range(DC):
            ins = gp.dma_start(out=gatew_s[:, dcc * 8:dcc * 8 + 8],
                               in_=gatew_d[dcc * 128:(dcc + 1) * 128, :])
            ctx.inc("gpsimd", "sw", ins, n=16)
        ctx.ev["gatew"] = ctx.cnt["sw"]

        def qkv_eslice_dma(g, kind, pe_gate=None):
            base_col = (g * 3 + kind) * 512
            slot = (g * 3 + kind) % 2
            if pe_gate is not None:
                ctx.wait("gpsimd", "pe", pe_gate)
            for dcc in range(DC):
                ins = gp.dma_start(
                    out=wring[:, slot * 8192 + dcc * 512:
                              slot * 8192 + dcc * 512 + 512],
                    in_=wqkvt_d[dcc * 128:(dcc + 1) * 128,
                                base_col:base_col + 512])
                ctx.inc("gpsimd", "sw", ins, n=16)
            ctx.ev[f"wqkv_{g}_{kind}"] = ctx.cnt["sw"]

        qkv_eslice_dma(0, 0)
        qkv_eslice_dma(0, 1)

        _LNB = [2, 10, 80, 88]

        def ln_stats(tile_ap, tag, s):
            b = _LNB[s]
            ins = ve.reduce_sum(smalls[:, b:b + 1], tile_ap, axis=AX.X,
                                negate=True)
            ctx.inc("vector", "ve", ins, ev=f"{tag}_sum")
            ctx.wait("vector", "ve", f"{tag}_sum")
            ins = ve.tensor_scalar_mul(smalls[:, b + 1:b + 2],
                                       smalls[:, b:b + 1], 1.0 / D)
            ctx.inc("vector", "ve", ins, ev=f"{tag}_mean")
            ctx.wait("scalar", "ve", f"{tag}_mean")
            ac.activation(out=vartrash[:, 0:D], in_=tile_ap, func=AF.Square,
                          scale=1.0, bias=smalls[:, b + 1:b + 2],
                          accum_out=smalls[:, b + 2:b + 3])
            ac.activation(out=smalls[:, b + 3:b + 4], in_=smalls[:, b + 2:b + 3],
                          func=AF.Ln, scale=1.0 / D, bias=eps_ap)
            ins = ac.activation(out=smalls[:, b + 4:b + 5],
                                in_=smalls[:, b + 3:b + 4],
                                func=AF.Exp, scale=-0.5)
            ctx.inc("scalar", "ac", ins, ev=f"{tag}_rstd")
            return smalls[:, b + 1:b + 2], smalls[:, b + 4:b + 5]

        ln1v = ln1fm[:].rearrange("p (c t) -> p c t", t=S)

        def wtile(g, kind, dcc):
            slot = (g * 3 + kind) % 2
            return wring[:, slot * 8192 + dcc * 512:
                         slot * 8192 + dcc * 512 + 512]

        def emit_q(g, rng=range(8)):
            if rng.start == 0:
                ctx.wait("tensor", "sw", f"wqkv_{g}_0")
            for nacc in rng:
                esub, tt = nacc // 2, nacc % 2
                ps = ps_s0 if nacc % 2 == 0 else ps_s1
                half = ((nacc // 2) % 2) * 512
                pap = ps[:, half:half + 512]
                if nacc >= 4:
                    ctx.wait("tensor", "ac", f"qev_{g}_{nacc-4}")
                elif g > 0:
                    ctx.wait("tensor", "ac", f"a_{g*4-1}_3_exp{2 + nacc % 2}")
                else:
                    ctx.wait("tensor", "ve", f"ln1e_{4*tt+3}_1")
                for dcc in range(DC):
                    ins = pe.matmul(
                        pap, wtile(g, 0, dcc)[:, esub * 128:esub * 128 + 128],
                        ln1v[:, dcc, tt * 512:tt * 512 + 512],
                        start=(dcc == 0), stop=(dcc == DC - 1))
                ctx.inc("tensor", "pe", ins, ev=f"qmm_{g}_{nacc}")
                ctx.wait("scalar", "pe", f"qmm_{g}_{nacc}")
                dst = qg[:, esub * HALF + tt * 512:esub * HALF + tt * 512 + 512]
                ins = ac.activation(out=dst, in_=pap, func=AF.Copy,
                                    scale=INV_SQRT_DH)
                ctx.inc("scalar", "ac", ins, ev=f"qev_{g}_{nacc}")

        def emit_k(g, rng):
            if rng.start == 0:
                ctx.wait("tensor", "sw", f"wqkv_{g}_1")
                qkv_eslice_dma(g, 2, pe_gate=f"qmm_{g}_7")
            for kk in rng:
                tt, esub = kk // 4, kk % 4
                ps = ps_s0 if kk % 2 == 0 else ps_s1
                half = ((kk // 2) % 2) * 512
                pap = ps[:, half:half + 512]
                if g == 0:
                    ctx.wait("tensor", "ve", f"ln1e_{4*tt+3}_1")
                if kk >= 4:
                    ctx.wait("tensor", "ve", f"kev_{g}_{kk-4}")
                else:
                    ctx.wait("tensor", "ac", f"qev_{g}_{4+kk}")
                for dcc in range(DC):
                    ins = pe.matmul(
                        pap, wtile(g, 1, dcc)[:, esub * 128:esub * 128 + 128],
                        ln1v[:, dcc, tt * 512:tt * 512 + 512],
                        start=(dcc == 0), stop=(dcc == DC - 1))
                ctx.inc("tensor", "pe", ins, ev=f"kmm_{g}_{kk}")
                ctx.wait("vector", "pe", f"kmm_{g}_{kk}")
                dst = kg[:, esub * S + tt * 512:esub * S + tt * 512 + 512]
                ins = ve.tensor_copy(dst, pap)
                ctx.inc("vector", "ve", ins, ev=f"kev_{g}_{kk}")

        ln1_mr = {}

        def ln1_stage1(i):
            tile = x0ring[:, (i % 2) * D:(i % 2) * D + D]
            ctx.wait("vector", "sx", f"x{i}")
            if i == 0:
                ctx.wait("vector", "ve", "init_ve")
            ln1_mr[i] = ln_stats(tile, f"ln1_{i}", i % 2)

        def ln1_stage2(i):
            tile = x0ring[:, (i % 2) * D:(i % 2) * D + D]
            lslot = lnscr[:, (i % 2) * D:(i % 2) * D + D]
            m, r = ln1_mr[i]
            ctx.wait("vector", "ac", f"ln1_{i}_rstd")
            if i >= 2:
                ctx.wait("vector", "pe", f"ln1t_{i-2}")
            ins = ve.tensor_scalar(lslot, tile, m, r,
                                   op0=ALU.add, op1=ALU.mult)
            ctx.inc("vector", "ve", ins, ev=f"ln1_{i}_norm")
            if i + 2 < 16:
                eng, engn = (sy, "sync") if i % 2 == 0 else (ac, "scalar")
                ctx.wait(engn, "ve", f"ln1_{i}_norm")
                ins = eng.dma_start(out=x0ring[:, (i % 2) * D:(i % 2) * D + D],
                                    in_=x_t[i + 2])
                ctx.inc(engn, "sx", ins, ev=f"x{i+2}", n=16)
            ctx.wait("tensor", "ve", f"ln1_{i}_norm")
            if i == 0:
                ctx.wait("tensor", "gp", "ident")
            for w in range(2):
                if i > 0 or w > 0:
                    ctx.wait("tensor", "ve",
                             f"ln1e_{i}_{w-1}" if w > 0 else f"ln1e_{i-1}_1")
                for bkk in range(8):
                    dcc = w * 8 + bkk
                    ins = pe.transpose(
                        ps_t[:, bkk * 128:bkk * 128 + 128],
                        lslot.rearrange("p (c f) -> p c f", f=128)[:, dcc],
                        ident[:])
                ctx.inc("tensor", "pe", ins,
                        ev=f"ln1t_{i}" if w == 1 else f"ln1tw_{i}_{w}")
                ctx.wait("vector", "pe",
                         f"ln1t_{i}" if w == 1 else f"ln1tw_{i}_{w}")
                dst = ln1fm[:].rearrange("p (c t) -> p c t", t=S)[
                    :, w * 8:w * 8 + 8, i * 128:(i + 1) * 128]
                ins = ve.tensor_copy(
                    dst, ps_t[:].rearrange("p (c f) -> p c f", f=128))
                ctx.inc("vector", "ve", ins, ev=f"ln1e_{i}_{w}")

        for i in range(17):
            if i < 16:
                ln1_stage1(i)
            if i >= 1:
                ln1_stage2(i - 1)
            if i == 8:
                emit_q(0, range(0, 4))
            if i == 11:
                emit_q(0, range(4, 8))
            if i == 13:
                emit_k(0, range(0, 4))
            if i == 15:
                emit_k(0, range(4, 8))

        if DEBUG_PHASE == 1:
            _dbg(ctx, sy, dbg_d, ln1fm, ("ve", "ln1e_15_1"))
            _epilogue(ctx, sy, gp, pe, ve, ac)
            return

        for g in range(NG):
            if g > 0:
                emit_q(g)
                emit_k(g, range(16))
            else:
                emit_k(0, range(8, 16))

            ctx.wait("tensor", "sw", f"wqkv_{g}_2")
            if g + 1 < NG:
                qkv_eslice_dma(g + 1, 0, pe_gate=f"kmm_{g}_15")
                qkv_eslice_dma(g + 1, 1, pe_gate=f"vmm_{g}_15")
            for tt in range(16):
                ps = ps_s0 if tt % 2 == 0 else ps_s1
                half = ((tt // 2) % 2) * 512
                pap = ps[:, half:half + 512]
                if g == 0:
                    ctx.wait("tensor", "ve", f"ln1e_{tt}_1")
                if tt >= 4:
                    ctx.wait("tensor", "ve", f"vev_{g}_{tt-4}")
                else:
                    ctx.wait("tensor", "ve", f"kev_{g}_{12+tt}")
                for dcc in range(DC):
                    ins = pe.matmul(
                        pap, ln1v[:, dcc, tt * 128:tt * 128 + 128],
                        wtile(g, 2, dcc),
                        start=(dcc == 0), stop=(dcc == DC - 1))
                ctx.inc("tensor", "pe", ins, ev=f"vmm_{g}_{tt}")
                ctx.wait("vector", "pe", f"vmm_{g}_{tt}")
                dst = vg[:].rearrange("p (h t) -> p h t", t=VHS)[
                    :, :, tt * VT:tt * VT + 128]
                ins = ve.tensor_copy(
                    dst, pap.rearrange("p (h t) -> p h t", t=128))
                ctx.inc("vector", "ve", ins, ev=f"vev_{g}_{tt}")

            for h in range(GH):
                head = g * GH + h
                for qc in range(4):
                    tag = f"a_{head}_{qc}"
                    prev = (f"a_{head}_{qc-1}" if qc > 0 else
                            (f"a_{head-1}_3" if h > 0 else None))
                    for wv in range(4):
                        ps = ps_s0 if wv % 2 == 0 else ps_s1
                        if wv >= 2:
                            ctx.wait("tensor", "ac", f"{tag}_exp{wv-2}")
                        elif prev is not None:
                            ctx.wait("tensor", "ac", f"{prev}_exp{wv+2}")
                        else:
                            ctx.wait("tensor", "ve", f"vev_{g}_{14+wv}")
                        for kt4 in range(4):
                            kt = wv * 4 + kt4
                            ins = pe.matmul(
                                ps[:, kt4 * 256:kt4 * 256 + 256],
                                kg[:, h * S + kt * 128:h * S + kt * 128 + 128],
                                qg[:, h * HALF + qc * 256:
                                   h * HALF + qc * 256 + 256],
                                start=True, stop=True)
                        ctx.inc("tensor", "pe", ins, ev=f"{tag}_st{wv}")
                        ctx.wait("scalar", "pe", f"{tag}_st{wv}")
                        if wv == 0:
                            if prev is not None:
                                ctx.wait("scalar", "pe", f"avdone_{head}_{qc-1}"
                                         if qc > 0 else f"avdone_{head-1}_3")
                            elif g > 0:
                                ctx.wait("scalar", "pe", f"avdone_{g*4-1}_3")
                        ins = ac.activation(
                            out=at_b[:, wv * 1024:wv * 1024 + 1024],
                            in_=ps[:], func=AF.Exp, bias=ebias_ap, scale=1.0)
                        ctx.inc("scalar", "ac", ins, ev=f"{tag}_exp{wv}")
                    for qs in range(2):
                        opsum = ps_3 if qs == 0 else ps_d
                        if prev is not None or g > 0:
                            pv = prev if prev is not None else f"a_{g*4-1}_3"
                            ctx.wait("tensor", "ve", f"{pv}_oev{qs}")
                        for kt in range(16):
                            if qs == 0 and kt % 4 == 0:
                                ctx.wait("tensor", "ac", f"{tag}_exp{kt//4}")
                            ins = pe.matmul(
                                opsum[:, 0:129],
                                at_b[:, kt * 256 + qs * 128:
                                     kt * 256 + qs * 128 + 128],
                                vg[:, h * VHS + kt * VT:
                                   h * VHS + kt * VT + 129],
                                start=(kt == 0), stop=(kt == 15))
                        avev = (f"{tag}_av0" if qs == 0
                                else f"avdone_{head}_{qc}")
                        ctx.inc("tensor", "pe", ins, ev=avev)
                        ctx.wait("scalar", "pe", avev)
                        ac.activation(out=smalls[:, 16:17],
                                      in_=opsum[:, 128:129], func=AF.Ln)
                        ins = ac.activation(out=smalls[:, 8 + qs:9 + qs],
                                            in_=smalls[:, 16:17],
                                            func=AF.Exp, scale=-1.0)
                        ctx.inc("scalar", "ac", ins, ev=f"{tag}_rc{qs}")
                        ctx.wait("vector", "ac", f"{tag}_rc{qs}")
                        ctx.wait("vector", "pe", avev)
                        qt = qc * 2 + qs
                        dst = o_all[:, (head * 8 + qt) * 128:
                                    (head * 8 + qt) * 128 + 128]
                        ins = ve.tensor_scalar(dst, opsum[:, 0:128],
                                               smalls[:, 8 + qs:9 + qs],
                                               None, op0=ALU.mult)
                        ctx.inc("vector", "ve", ins, ev=f"{tag}_oev{qs}")

        if DEBUG_PHASE == 3:
            _dbg(ctx, sy, dbg_d, o_all, ("ve", "a_15_3_oev1"))
            _epilogue(ctx, sy, gp, pe, ve, ac)
            return

        def wout_dma(ds, pe_gate):
            slot = ds % 2
            if pe_gate is not None:
                ctx.wait("gpsimd", "pe", pe_gate)
            for ec in range(DC):
                ins = gp.dma_start(
                    out=wring2[:, slot * 8192 + ec * 512:
                               slot * 8192 + ec * 512 + 512],
                    in_=woutt_d[ec * 128:(ec + 1) * 128,
                                ds * 512:(ds + 1) * 512])
                ctx.inc("gpsimd", "sw", ins, n=16)
            ctx.ev[f"wout_{ds}"] = ctx.cnt["sw"]

        wout_dma(0, "vmm_3_15")
        wout_dma(1, None)

        otseq = [(w, head) for w in range(2) for head in range(NH)]
        for i, (w, head) in enumerate(otseq):
            pst = ps_t if i % 2 == 0 else ps_t2
            if i >= 2:
                pw, ph = otseq[i - 2]
                ctx.wait("tensor", "ve", f"oev_tw_{pw}_{ph}")
            else:
                ctx.wait("tensor", "ve", "a_15_3_oev1")
            for bkk in range(4):
                qt = w * 4 + bkk
                ins = pe.transpose(
                    pst[:, bkk * 128:bkk * 128 + 128],
                    o_all[:, (head * 8 + qt) * 128:
                          (head * 8 + qt) * 128 + 128],
                    ident[:])
            ctx.inc("tensor", "pe", ins, ev=f"ot_{w}_{head}")
            ctx.wait("vector", "pe", f"ot_{w}_{head}")
            dst = o_fm[:, head * HALF + w * 512:head * HALF + w * 512 + 512]
            ins = ve.tensor_copy(dst, pst[:, 0:512])
            ctx.inc("vector", "ve", ins, ev=f"oev_tw_{w}_{head}")

        op_sched = [(ds, tt) for pair in (0, 1) for tt in range(8)
                    for ds in (2 * pair, 2 * pair + 1)]
        for gi, (ds, tt) in enumerate(op_sched):
            if gi == 0:
                ctx.wait("tensor", "sw", "wout_0")
                ctx.wait("tensor", "sw", "wout_1")
            if gi == 16:
                ctx.wait("tensor", "sw", "wout_2")
                ctx.wait("tensor", "sw", "wout_3")
            ps = ps_s0 if gi % 2 == 0 else ps_s1
            half = ((gi // 2) % 2) * 512
            pap = ps[:, half:half + 512]
            if gi >= 4:
                pds, ptt = op_sched[gi - 4]
                ctx.wait("tensor", "ve", f"opev_{pds * 8 + ptt}")
            elif gi == 0:
                ctx.wait("tensor", "ve", "oev_tw_0_15")
            if gi == 8:
                ctx.wait("tensor", "ve", "oev_tw_1_15")
            for ec in range(DC):
                ins = pe.matmul(
                    pap,
                    o_fm[:, ec * HALF + tt * 128:ec * HALF + tt * 128 + 128],
                    wring2[:, (ds % 2) * 8192 + ec * 512:
                           (ds % 2) * 8192 + ec * 512 + 512],
                    start=(ec == 0), stop=(ec == DC - 1))
            ctx.inc("tensor", "pe", ins, ev=f"opmm_{ds * 8 + tt}")
            ctx.wait("vector", "pe", f"opmm_{ds * 8 + tt}")
            dst = attout[:, tt * D + ds * 512:tt * D + ds * 512 + 512]
            ins = ve.tensor_copy(dst, pap)
            ctx.inc("vector", "ve", ins, ev=f"opev_{ds * 8 + tt}")
            if gi == 14:
                wout_dma(2, "opmm_7")
            if gi == 15:
                wout_dma(3, "opmm_15")

        if DEBUG_PHASE == 4:
            _dbg(ctx, sy, dbg_d, attout, ("ve", "opev_31"))
            _epilogue(ctx, sy, gp, pe, ve, ac)
            return

        lnA_mr = {}
        ln2_mr = {}
        ln2v = ln2fm[:].rearrange("p (c t) -> p c t", t=HALF)

        def x1_stageA(tt):
            eng, engn = (sy, "sync") if tt % 2 == 0 else (ac, "scalar")
            ctx.wait(engn, "ve", "ln1_15_norm")
            if tt >= 2:
                ctx.wait(engn, "ve", f"x1_{tt-2}_done")
            ins = eng.dma_start(out=x0ring[:, (tt % 2) * D:(tt % 2) * D + D],
                                in_=x_t[tt])
            ctx.inc(engn, "sx", ins, ev=f"x0b_{tt}", n=16)
            tile = attout[:, tt * D:(tt + 1) * D]
            for ds in range(NDS):
                ctx.wait("vector", "ve", f"opev_{ds*8+tt}")
            lnA_mr[tt] = ln_stats(tile, f"lnA_{tt}", tt % 2)

        def x1_stageB(tt):
            tile = attout[:, tt * D:(tt + 1) * D]
            m, r = lnA_mr[tt]
            ctx.wait("vector", "ac", f"lnA_{tt}_rstd")
            ins = ve.tensor_scalar(tile, tile, m, r,
                                   op0=ALU.add, op1=ALU.mult)
            ctx.inc("vector", "ve", ins, ev=f"x1n_{tt}")
            ctx.wait("vector", "sx", f"x0b_{tt}")
            ctx.wait("vector", "ve", f"x1n_{tt}")
            ins = ve.tensor_add(tile, tile,
                                x0ring[:, (tt % 2) * D:(tt % 2) * D + D])
            ctx.inc("vector", "ve", ins, ev=f"x1_{tt}_done")
            ctx.wait("sync", "ve", f"x1_{tt}_done")
            ins = sy.dma_start(out=x1_t[tt], in_=tile)
            ctx.inc("sync", "st", ins, ev=f"x1st_{tt}", n=16)
            ctx.wait("vector", "ve", f"x1_{tt}_done")
            ln2_mr[tt] = ln_stats(tile, f"ln2_{tt}", 2 + tt % 2)

        def x1_stageC(tt):
            tile = attout[:, tt * D:(tt + 1) * D]
            lslot = lnscr[:, (tt % 2) * D:(tt % 2) * D + D]
            m, r = ln2_mr[tt]
            ctx.wait("vector", "ac", f"ln2_{tt}_rstd")
            if tt >= 2:
                ctx.wait("vector", "pe", f"ln2t_{tt-2}")
            ins = ve.tensor_scalar(lslot, tile, m, r,
                                   op0=ALU.add, op1=ALU.mult)
            ctx.inc("vector", "ve", ins, ev=f"ln2_{tt}_norm")
            ctx.wait("tensor", "ve", f"ln2_{tt}_norm")
            for w in range(2):
                if tt > 0 or w > 0:
                    ctx.wait("tensor", "ve",
                             f"ln2e_{tt}_{w-1}" if w > 0 else f"ln2e_{tt-1}_1")
                for bkk in range(8):
                    dcc = w * 8 + bkk
                    ins = pe.transpose(
                        ps_t[:, bkk * 128:bkk * 128 + 128],
                        lslot.rearrange("p (c f) -> p c f", f=128)[:, dcc],
                        ident[:])
                ctx.inc("tensor", "pe", ins,
                        ev=f"ln2t_{tt}" if w == 1 else f"ln2tw_{tt}_{w}")
                ctx.wait("vector", "pe",
                         f"ln2t_{tt}" if w == 1 else f"ln2tw_{tt}_{w}")
                dst = ln2fm[:].rearrange("p (c t) -> p c t", t=HALF)[
                    :, w * 8:w * 8 + 8, tt * 128:(tt + 1) * 128]
                ins = ve.tensor_copy(
                    dst, ps_t[:].rearrange("p (c f) -> p c f", f=128))
                ctx.inc("vector", "ve", ins, ev=f"ln2e_{tt}_{w}")
            if tt == 0:
                ctx.wait("tensor", "sw", "gatew")
            if tt >= 2:
                ctx.wait("tensor", "ve", f"comb_{tt-2}_done")
            ctx.wait("tensor", "ve", f"ln2e_{tt}_1")
            lgp = ps_2 if tt % 2 == 0 else ps_3
            for dcc in range(DC):
                ins = pe.matmul(lgp[:, 0:8],
                                ln2v[:, dcc, tt * 128:(tt + 1) * 128],
                                gatew_s[:, dcc * 8:dcc * 8 + 8],
                                start=(dcc == 0), stop=(dcc == DC - 1))
            ctx.inc("tensor", "pe", ins, ev=f"lg_{tt}")
            pl = lgp[:, 0:8]
            tc = f"cmb_{tt}"
            ctx.wait("vector", "pe", f"lg_{tt}")
            ins = ve.reduce_max(smalls[:, 40:41], pl, axis=AX.X, negate=True)
            ctx.inc("vector", "ve", ins, ev=f"{tc}_m")
            ctx.wait("scalar", "ve", f"{tc}_m")
            ins = ac.activation(out=smalls[:, 48:56], in_=pl, func=AF.Exp,
                                bias=smalls[:, 40:41], scale=1.0,
                                accum_out=smalls[:, 41:42])
            ctx.inc("scalar", "ac", ins, ev=f"{tc}_e")
            ctx.wait("vector", "ac", f"{tc}_e")
            ins = ve.reciprocal(smalls[:, 42:43], smalls[:, 41:42])
            ctx.inc("vector", "ve", ins, ev=f"{tc}_r")
            ctx.wait("vector", "ve", f"{tc}_r")
            ins = ve.tensor_scalar(smalls[:, 48:56], smalls[:, 48:56],
                                   smalls[:, 42:43], None, op0=ALU.mult)
            ctx.inc("vector", "ve", ins, ev=f"{tc}_s")
            ctx.wait("vector", "ve", f"{tc}_s")
            ins = ve.reduce_max(smalls[:, 43:44], smalls[:, 48:56], axis=AX.X)
            ctx.inc("vector", "ve", ins, ev=f"{tc}_m1")
            ctx.wait("vector", "ve", f"{tc}_m1")
            ins = ve.tensor_scalar(smalls[:, 56:64], smalls[:, 48:56],
                                   smalls[:, 43:44], None, op0=ALU.is_equal)
            ctx.inc("vector", "ve", ins, ev=f"{tc}_eq")
            ctx.wait("vector", "ve", f"{tc}_eq")
            ins = ve.scalar_tensor_tensor(smalls[:, 56:64], smalls[:, 56:64],
                                          -1e30, smalls[:, 48:56],
                                          op0=ALU.mult, op1=ALU.add)
            ctx.inc("vector", "ve", ins, ev=f"{tc}_s2")
            ctx.wait("vector", "ve", f"{tc}_s2")
            ins = ve.reduce_max(smalls[:, 44:45], smalls[:, 56:64], axis=AX.X)
            ctx.inc("vector", "ve", ins, ev=f"{tc}_m2")
            ctx.wait("vector", "ve", f"{tc}_m2")
            ins = ve.tensor_scalar(smalls[:, 56:64], smalls[:, 48:56],
                                   smalls[:, 44:45], None, op0=ALU.is_ge)
            ctx.inc("vector", "ve", ins, ev=f"{tc}_ge")
            ctx.wait("vector", "ve", f"{tc}_ge")
            ins = ve.tensor_mul(combtm[:, tt * 8:tt * 8 + 8],
                                smalls[:, 48:56], smalls[:, 56:64])
            ctx.inc("vector", "ve", ins, ev=f"comb_{tt}_done")

        for tt in range(10):
            if tt < 8:
                x1_stageA(tt)
            if 1 <= tt <= 8:
                x1_stageB(tt - 1)
            if tt >= 2:
                x1_stageC(tt - 2)

        if DEBUG_PHASE == 5:
            _dbg(ctx, sy, dbg_d, ln2fm, ("ve", "ln2e_7_1"))
            _epilogue(ctx, sy, gp, pe, ve, ac)
            return

        if DEBUG_PHASE == 6:
            _dbg(ctx, sy, dbg_d, combtm, ("ve", "comb_7_done"))
            _epilogue(ctx, sy, gp, pe, ve, ac)
            return

        def wgu_dma(e, sw, which, pe_gate):
            src = wg_d if which == 0 else wu_d
            w = SW_HCS[sw]
            if pe_gate is not None:
                ctx.wait("gpsimd", "pe", pe_gate)
            seg = src[e, sw_off[sw]:sw_off[sw + 1]].rearrange(
                "(c p n) -> p c n", p=128, n=w * 128)
            ins = gp.dma_start(
                out=wgring[:, which * 6144:which * 6144 + DC * w * 128]
                .rearrange("p (c n) -> p c n", n=w * 128),
                in_=seg)
            ctx.inc("gpsimd", "sw", ins, n=16, ev=f"wgu_{e}_{sw}_{which}")

        def wdn_dma(e, ds, pe_gate):
            slot = (e * 4 + ds) % 2
            if pe_gate is not None:
                ctx.wait("gpsimd", "pe", pe_gate)
            seg = wdn_d[e, ds].rearrange("(c p) n -> p c n", p=128)
            ins = gp.dma_start(
                out=wdnring[:, slot * NHC * 512:(slot + 1) * NHC * 512]
                .rearrange("p (c n) -> p c n", n=512),
                in_=seg)
            ctx.inc("gpsimd", "sw", ins, n=16, ev=f"wdn_{e}_{ds}")

        wgu_dma(0, 0, 0, "opmm_31")
        wgu_dma(0, 0, 1, None)
        wdn_dma(0, 0, None)
        wdn_dma(0, 1, None)

        def sweep_tile(i, half):
            if i == 0:
                return ps_s0[:, half * 512:half * 512 + 512]
            if i == 1:
                return ps_s1[:, half * 512:half * 512 + 512]
            return (ps_d if half == 0 else ps_d2)[:, 0:512]

        def gslot(e):
            return gbuf[:, 0:NHC * HALF]

        lnF_mr = {}

        def fin_stageA(tt):
            eng, engn = (sy, "sync") if tt % 2 == 0 else (ac, "scalar")
            if tt < 2:
                ctx.wait(engn, "ve", f"x1_{6+tt}_done")
            else:
                ctx.wait(engn, "ve", f"fin_{tt-2}_done")
            ctx.wait(engn, "st", f"x1st_{tt}")
            ins = eng.dma_start(out=x0ring[:, (tt % 2) * D:(tt % 2) * D + D],
                                in_=x1_t[tt])
            ctx.inc(engn, "sx", ins, ev=f"x1rd_{tt}", n=16)
            tile = ffnacc[:, tt * D:(tt + 1) * D]
            for ds in range(NDS):
                ctx.wait("vector", "ve", f"dnev_{NEP-1}_{ds}_{tt}")
            lnF_mr[tt] = ln_stats(tile, f"lnF_{tt}", tt % 2)

        def fin_stageB(tt):
            tile = ffnacc[:, tt * D:(tt + 1) * D]
            m, r = lnF_mr[tt]
            ctx.wait("vector", "ac", f"lnF_{tt}_rstd")
            ins = ve.tensor_scalar(tile, tile, m, r,
                                   op0=ALU.add, op1=ALU.mult)
            ctx.inc("vector", "ve", ins, ev=f"finn_{tt}")
            ctx.wait("vector", "sx", f"x1rd_{tt}")
            ctx.wait("vector", "ve", f"finn_{tt}")
            ins = ve.tensor_add(tile, tile,
                                x0ring[:, (tt % 2) * D:(tt % 2) * D + D])
            ctx.inc("vector", "ve", ins, ev=f"fin_{tt}_done")
            eng, engn = (sy, "sync") if tt % 2 == 0 else (ac, "scalar")
            ctx.wait(engn, "ve", f"fin_{tt}_done")
            ins = eng.dma_start(out=out_t[tt], in_=tile)
            ctx.inc(engn, "st", ins, n=16)

        for e in range(NEP):
            for sw in range(4):
                w = SW_HCS[sw]
                hbase = sum(SW_HCS[:sw])

                def gucol(hcl, half):
                    return gslot(e)[:, (hbase + hcl) * HALF + half * 512:
                                    (hbase + hcl) * HALF + half * 512 + 512]

                ctx.wait("tensor", "sw", f"wgu_{e}_{sw}_0")
                for half in range(2):
                    if e == 0 and sw == 0:
                        ctx.wait("tensor", "ve", f"ln2e_{4*half+3}_1")
                    elif sw == 0:
                        ctx.wait("tensor", "ve", f"guev_{e-1}_3_{half}")
                        if half == 0:
                            ctx.wait("tensor", "ve", f"dnev_{e-1}_3_7")
                    else:
                        ctx.wait("tensor", "ve", f"guev_{e}_{sw-1}_{half}")
                    for dcc in range(DC):
                        for hcl in range(w):
                            ins = pe.matmul(
                                sweep_tile(hcl, half),
                                wgring[:, 0 * 6144 + (dcc * w + hcl) * 128:
                                       0 * 6144 + (dcc * w + hcl) * 128 + 128],
                                ln2v[:, dcc, half * 512:half * 512 + 512],
                                start=(dcc == 0), stop=(dcc == DC - 1))
                    ctx.inc("tensor", "pe", ins, ev=f"gmm_{e}_{sw}_{half}")
                    ctx.wait("scalar", "pe", f"gmm_{e}_{sw}_{half}")
                    for hcl in range(w):
                        ins = ac.activation(out=gucol(hcl, half),
                                            in_=sweep_tile(hcl, half),
                                            func=AF.Silu)
                    ctx.inc("scalar", "ac", ins, ev=f"silu_{e}_{sw}_{half}")
                if sw + 1 < 4:
                    wgu_dma(e, sw + 1, 0, f"gmm_{e}_{sw}_1")
                elif e + 1 < NEP:
                    wgu_dma(e + 1, 0, 0, f"gmm_{e}_{sw}_1")
                ctx.wait("tensor", "sw", f"wgu_{e}_{sw}_1")
                for half in range(2):
                    ctx.wait("tensor", "ac", f"silu_{e}_{sw}_{half}")
                    for dcc in range(DC):
                        for hcl in range(w):
                            ins = pe.matmul(
                                sweep_tile(hcl, half),
                                wgring[:, 6144 + (dcc * w + hcl) * 128:
                                       6144 + (dcc * w + hcl) * 128 + 128],
                                ln2v[:, dcc, half * 512:half * 512 + 512],
                                start=(dcc == 0), stop=(dcc == DC - 1))
                    ctx.inc("tensor", "pe", ins, ev=f"umm_{e}_{sw}_{half}")
                    ctx.wait("vector", "pe", f"umm_{e}_{sw}_{half}")
                    ctx.wait("vector", "ac", f"silu_{e}_{sw}_{half}")
                    for hcl in range(w):
                        ins = ve.tensor_mul(gucol(hcl, half), gucol(hcl, half),
                                            sweep_tile(hcl, half))
                    ctx.inc("vector", "ve", ins, ev=f"guev_{e}_{sw}_{half}")
                if sw + 1 < 4:
                    wgu_dma(e, sw + 1, 1, f"umm_{e}_{sw}_1")
                elif e + 1 < NEP:
                    wgu_dma(e + 1, 0, 1, f"umm_{e}_{sw}_1")

            gdone = f"guev_{e}_3_1"

            if e < NEP - 1:
                sched = [(ds, tt) for ds in range(NDS) for tt in range(8)]
            else:
                sched = ([(0, tt) for tt in range(8)] +
                         [(1, tt) for tt in range(8)] +
                         [(ds, tt) for tt in range(8) for ds in (2, 3)])
            seen_ds = set()
            for gi, (ds, tt) in enumerate(sched):
                if ds not in seen_ds:
                    seen_ds.add(ds)
                    ctx.wait("tensor", "sw", f"wdn_{e}_{ds}")
                if gi == 0:
                    ctx.wait("tensor", "ve", gdone)
                    if e == 0:
                        ctx.wait("vector", "st", "x1st_7")
                dpsum = ps_d if gi % 2 == 0 else ps_d2
                if e > 0 or gi >= 2:
                    if gi >= 2:
                        pds, ptt = sched[gi - 2]
                        pee = e
                    else:
                        pds, ptt = 3, 6 + gi
                        pee = e - 1
                        if e == NEP - 1:
                            pds, ptt = (3, 7) if gi == 1 else (3, 6)
                    ctx.wait("tensor", "ve", f"dnev_{pee}_{pds}_{ptt}")
                for hc in range(NHC):
                    ins = pe.matmul(
                        dpsum[:],
                        gslot(e)[:, hc * HALF + tt * 128:
                                 hc * HALF + tt * 128 + 128],
                        wdnring[:, ((e * 4 + ds) % 2) * NHC * 512 + hc * 512:
                                ((e * 4 + ds) % 2) * NHC * 512 + hc * 512 + 512],
                        start=(hc == 0), stop=(hc == NHC - 1))
                ctx.inc("tensor", "pe", ins, ev=f"dnmm_{e}_{ds}_{tt}")
                ctx.wait("vector", "pe", f"dnmm_{e}_{ds}_{tt}")
                dst = ffnacc[:, tt * D + ds * 512:tt * D + ds * 512 + 512]
                cap = (combtm[:, tt * 8 + e:tt * 8 + e + 1]
                       if e < 8 else None)
                if e == 0:
                    ins = ve.tensor_scalar(dst, dpsum[:], cap, None,
                                           op0=ALU.mult)
                elif e < 8:
                    ins = ve.scalar_tensor_tensor(dst, dpsum[:], cap, dst,
                                                  op0=ALU.mult, op1=ALU.add)
                else:
                    ins = ve.tensor_add(dst, dst, dpsum[:])
                ctx.inc("vector", "ve", ins, ev=f"dnev_{e}_{ds}_{tt}")
                if e < NEP - 1:
                    if tt == 7:
                        nxt_e, nxt_ds = e, ds + 2
                        if nxt_ds >= NDS:
                            nxt_e, nxt_ds = e + 1, nxt_ds - NDS
                        if nxt_e == NEP - 1 or nxt_ds < NDS:
                            if nxt_e < NEP:
                                wdn_dma(nxt_e, nxt_ds, f"dnmm_{e}_{ds}_7")
                else:
                    if gi == 7:
                        wdn_dma(e, 2, f"dnmm_{e}_0_7")
                    if gi == 15:
                        wdn_dma(e, 3, f"dnmm_{e}_1_7")
                    if gi >= 16 and ds == 3:
                        fin_stageA(tt)
                        if tt >= 1:
                            fin_stageB(tt - 1)

        if DEBUG_PHASE == 7:
            for ds in range(NDS):
                for tt in range(8):
                    ctx.wait("sync", "ve", f"dnev_{NEP-1}_{ds}_{tt}")
            _dbg(ctx, sy, dbg_d, ffnacc, ("ve", f"dnev_{NEP-1}_3_7"))
            _epilogue(ctx, sy, gp, pe, ve, ac)
            return

        fin_stageB(7)
        _epilogue(ctx, sy, gp, pe, ve, ac)

    def _dbg(ctx, sy, dbg, buf, wait, rows=128):
        ctx.wait("sync", wait[0], wait[1])
        cols = buf.shape[1]
        n32 = cols if buf.dtype == F32 else cols // 2
        ins = sy.dma_start(out=dbg[0:rows, 0:n32],
                           in_=buf[0:rows, 0:cols].bitcast(F32))
        ctx.inc("sync", "st", ins, n=16)

    def _epilogue(ctx, sy, gp, pe, ve, ac):
        ve_final = ctx.cnt["ve"]
        ins = ve.tensor_copy(smalls[:, 46:47], smalls[:, 20:21])
        ctx.inc("vector", "dn", ins)
        ins = ac.copy(smalls[:, 47:48], smalls[:, 20:21])
        ctx.inc("scalar", "dn", ins)
        ctx.wait("tensor", "ve", ve_final)
        ins = pe.matmul(ps_2[0:1, 400:401], smalls[0:1, 20:21], onesrow_f,
                        start=True, stop=True)
        ctx.inc("tensor", "dn", ins)
        ins = gp.memset(onesb[:, 8:9], 0.0)
        ctx.inc("gpsimd", "dn", ins)
        ctx.wait("sync", "st", ctx.cnt["st"])
        ctx.wait("sync", "dn", ctx.cnt["dn"])
        ctx.wait("sync", "sx", ctx.cnt["sx"])
        ctx.wait("sync", "sw", ctx.cnt["sw"])
        if ctx.cur == "sync":
            for s_ in sem_names:
                ctx.nc.sync.sem_clear(sems[s_])

    from contextlib import ExitStack
    es = ExitStack()
    sems = {}
    for s_ in sem_names:
        sems[s_] = es.enter_context(nc.semaphore(s_))

    ctx = Ctx(nc, sems)
    ctx.begin(None)
    program(ctx)

    block_cm = nc.Block()
    block = block_cm.__enter__()

    def mk(engname):
        def body(_eng):
            ctx.begin(engname)
            for s_ in ctx.waits.get(engname, []):
                getattr(nc, engname).sem_clear(sems[s_])
            program(ctx)
        return body

    block.sync(mk("sync"))
    block.gpsimd(mk("gpsimd"))
    block.vector(mk("vector"))
    block.scalar(mk("scalar"))
    block.tensor(mk("tensor"))
    block_cm.__exit__(None, None, None)
    es.close()
    return nc



def _pack_weights(Wqkv, Wout, gate_w, w_gate, w_up, w_down, sw_gate, sw_up,
                  sw_down):
    WqkvT = np.ascontiguousarray(np.asarray(Wqkv, np.float32).T)
    cols = []
    for g in range(NG):
        cols += list(range(g * 512, g * 512 + 512))
        cols += list(range(2048 + g * 512, 2048 + g * 512 + 512))
        cols += list(range(4096 + g * 512, 4096 + g * 512 + 512))
    wqkvt = np.ascontiguousarray(WqkvT[:, cols])
    woutt = np.ascontiguousarray(np.asarray(Wout, np.float32).T)
    gatew = np.ascontiguousarray(np.asarray(gate_w, np.float32))
    swg = np.asarray(sw_gate, np.float32).reshape(D, 2, HFF).transpose(1, 0, 2)
    swu = np.asarray(sw_up, np.float32).reshape(D, 2, HFF).transpose(1, 0, 2)
    wg10 = np.concatenate([np.asarray(w_gate, np.float32), swg], axis=0)
    wu10 = np.concatenate([np.asarray(w_up, np.float32), swu], axis=0)

    def sweep_pack(m10):
        out = np.empty((NEP, D * HFF), np.float32)
        for e in range(NEP):
            m = m10[e]
            parts = []
            hb = 0
            for w in SW_HCS:
                seg = m[:, hb * 128:(hb + w) * 128]
                seg = np.ascontiguousarray(seg.reshape(DC, 128, w * 128))
                parts.append(seg.reshape(-1))
                hb += w
            out[e] = np.concatenate(parts)
        return np.ascontiguousarray(out)

    wg = sweep_pack(wg10)
    wu = sweep_pack(wu10)

    swd = np.asarray(sw_down, np.float32).reshape(2, HFF, D)
    wdn10 = np.concatenate([np.asarray(w_down, np.float32), swd], axis=0)
    wdn = np.ascontiguousarray(
        wdn10.reshape(NEP, HFF, NDS, 512).transpose(0, 2, 1, 3))
    return wqkvt, woutt, gatew, wg, wu, wdn


def kernel(L0_attn_x, L0_attn_Wqkv, L0_attn_Wout, L0_ffn_gate_w, L0_ffn_w_gate,
           L0_ffn_w_up, L0_ffn_w_down, L0_ffn_sw_gate, L0_ffn_sw_up,
           L0_ffn_sw_down):
    X = np.asarray(L0_attn_x, np.float32)
    wqkvt, woutt, gatew, wg, wu, wdn = _pack_weights(
        L0_attn_Wqkv, L0_attn_Wout, L0_ffn_gate_w, L0_ffn_w_gate, L0_ffn_w_up,
        L0_ffn_w_down, L0_ffn_sw_gate, L0_ffn_sw_up, L0_ffn_sw_down)

    nc = build_program()
    in_maps = []
    for c in range(8):
        b, j = c // 2, c % 2
        xb = X[b] if j == 0 else np.ascontiguousarray(
            np.roll(X[b], -HALF, axis=0))
        in_maps.append({"x": xb, "wqkvt": wqkvt, "woutt": woutt,
                        "gatew": gatew, "wg": wg, "wu": wu, "wdn": wdn})
    res = run_bass_kernel_spmd(nc, in_maps, core_ids=list(range(8)))
    out = np.empty((NB, S, D), np.float32)
    for c in range(8):
        b, j = c // 2, c % 2
        out[b, j * HALF:(j + 1) * HALF] = res.results[c]["out"]
    return out


# version 10 (final)
# speedup vs baseline: 1.1942x; 1.1942x over previous
import numpy as np
import ml_dtypes
import concourse.bass as bass
import concourse.mybir as mybir
from concourse.bass_utils import run_bass_kernel_spmd

F32 = mybir.dt.float32
BF16 = mybir.dt.bfloat16
AF = mybir.ActivationFunctionType
ALU = mybir.AluOpType
AX = mybir.AxisListType
BF = ml_dtypes.bfloat16

D = 2048
S = 2048
NB = 4
HALF = 1024
NH = 16
DH = 128
NG = 4
GH = 4
DC = 16
VT = 136
VHS = 16 * VT
HFF = 1408
NHC = 11
NEP = 10
NDS = 4
SW_HCS = (3, 3, 3, 2)
INV_SQRT_DH = float(1.0 / np.sqrt(128.0))
EXP_BIAS = -12.0
LN_EPS = 1e-5

DEBUG_PHASE = 0


class _Null:
    def __getattr__(self, k):
        return lambda *a, **kw: None


_NULL = _Null()


class Ctx:

    def __init__(self, nc, sems):
        self.nc = nc
        self.sems = sems
        self.ev = {}
        self.waits = {}
        self.cur = None
        self.cnt = {}

    def begin(self, cur):
        self.cur = cur
        self.cnt = {k: 0 for k in self.sems}

    def eng(self, name):
        return getattr(self.nc, name) if self.cur == name else _NULL

    def inc(self, engname, sem, ins, ev=None, n=1):
        self.cnt[sem] += n
        if ev is not None:
            self.ev[ev] = self.cnt[sem]
        if self.cur == engname and ins is not None:
            ins.then_inc(self.sems[sem], n)
        return self.cnt[sem]

    def wait(self, engname, sem, ev):
        w = self.waits.setdefault(engname, [])
        if sem not in w:
            w.append(sem)
        if self.cur == engname:
            v = self.ev[ev] if isinstance(ev, str) else ev
            getattr(self.nc, engname).wait_ge(self.sems[sem], v)


def build_program():
    nc = bass.Bass("TRN2", target_bir_lowering=False,
                   detect_race_conditions=False)

    x_d = nc.dram_tensor("x", [S, D], F32, kind="ExternalInput").ap()
    wqkvt_d = nc.dram_tensor("wqkvt", [D, 3 * D], F32, kind="ExternalInput").ap()
    woutt_d = nc.dram_tensor("woutt", [D, D], F32, kind="ExternalInput").ap()
    gatew_d = nc.dram_tensor("gatew", [D, 8], F32, kind="ExternalInput").ap()
    wg_d = nc.dram_tensor("wg", [NEP, D * HFF], F32, kind="ExternalInput").ap()
    wu_d = nc.dram_tensor("wu", [NEP, D * HFF], F32, kind="ExternalInput").ap()
    wdn_d = nc.dram_tensor("wdn", [NEP, NDS, HFF, 512], F32,
                           kind="ExternalInput").ap()
    out_d = nc.dram_tensor("out", [HALF, D], F32, kind="ExternalOutput").ap()
    x1_d = nc.dram_tensor("x1b", [HALF, D], F32, kind="Internal").ap()
    dbg_d = None
    if DEBUG_PHASE:
        dbg_d = nc.dram_tensor("dbg", [128, 32768], F32,
                               kind="ExternalOutput").ap()

    x_t = x_d.rearrange("(n p) d -> n p d", p=128)
    x1_t = x1_d.rearrange("(n p) d -> n p d", p=128)
    out_t = out_d.rearrange("(n p) d -> n p d", p=128)

    sw_off = [0]
    for w in SW_HCS:
        sw_off.append(sw_off[-1] + DC * 128 * (w * 128))

    BASE = 16512
    LIMIT = 229300
    ptr = [BASE]
    smap = {}

    def place(name, shape, dtype, at=None):
        nbytes = int(np.prod(shape[1:])) * mybir.dt.size(dtype)
        if at is None:
            o = (ptr[0] + 31) // 32 * 32
            ptr[0] = o + nbytes
            assert ptr[0] <= LIMIT, (name, ptr[0])
        else:
            o = smap[at][0] if isinstance(at, str) else at
        h = nc.alloc_sbuf_tensor_at(name, shape, dtype, offset=o)
        smap[name] = (o, nbytes)
        return h

    ident = place("ident", [128, 128], BF16)
    smalls = place("smalls", [128, 128], F32)
    onesb = place("onesb", [128, 16], BF16)
    onesrow_b = place("onesrow_b", [1, 128], BF16)
    gatew_s = place("gatew_s", [128, DC * 8], BF16)
    combtm = place("combtm", [128, 64], F32)
    o_all = place("o_all", [128, NH * 8 * 128], BF16)
    ln2fm = place("ln2fm", [128, DC * HALF], BF16, at="o_all")
    ln1fm = place("ln1fm", [128, DC * S], BF16)
    attout = place("attout", [128, 8 * D], F32, at="ln1fm")
    ffnacc = place("ffnacc", [128, 8 * D], F32, at="ln1fm")
    x0ring = place("x0ring", [128, 2 * D], F32)
    lnscr = place("lnscr", [128, 2 * D], BF16)
    qg = place("qg", [128, GH * HALF], BF16)
    kg = place("kg", [128, GH * S], BF16)
    vg = place("vg", [128, GH * VHS], BF16)
    at_b = place("at_b", [128, 16 * 256], BF16)
    wring = place("wring", [128, 2 * DC * 512], BF16)
    vartrash = place("vartrash", [128, D], BF16)
    o_fm = place("o_fm", [128, NH * HALF], BF16, at="kg")
    wring2 = place("wring2", [128, 2 * DC * 512], BF16, at="wring")
    gbuf = place("gbuf", [128, NHC * HALF], BF16, at="qg")
    cptr = [smap["qg"][0] + NHC * HALF * 2]

    def placeC(name, shape, dtype):
        nbytes = int(np.prod(shape[1:])) * mybir.dt.size(dtype)
        o = (cptr[0] + 31) // 32 * 32
        cptr[0] = o + nbytes
        h = nc.alloc_sbuf_tensor_at(name, shape, dtype, offset=o)
        smap[name] = (o, nbytes)
        return h

    wgring = placeC("wgring", [128, 2 * DC * 384], BF16)
    wdnring = placeC("wdnring", [128, 2 * NHC * 512], BF16)
    assert cptr[0] <= smap["wring"][0] + smap["wring"][1], (
        "C scratch overflow", cptr[0])

    ps_s0 = nc.alloc_psum_tensor("ps_s0", [128, 1024], F32)
    ps_s1 = nc.alloc_psum_tensor("ps_s1", [128, 1024], F32)
    ps_2 = nc.alloc_psum_tensor("ps_2", [128, 512], F32)
    ps_3 = nc.alloc_psum_tensor("ps_3", [128, 512], F32)
    mark0 = nc.psum_base
    ps_d = nc.alloc_psum_tensor("ps_d", [128, 512], F32)
    mark = nc.psum_base
    ps_t = nc.alloc_psum_tensor("ps_t", [128, 1024], BF16)
    nc.psum_base = mark0
    ps_t2 = nc.alloc_psum_tensor("ps_t2", [128, 1024], BF16)
    nc.psum_base = mark
    ps_d2 = nc.alloc_psum_tensor("ps_d2", [128, 512], F32)
    assert nc.psum_base <= 8

    eps_ap = smalls[:, 0:1]
    ebias_ap = smalls[:, 1:2]
    onescol_b = onesb[:, 0:1]
    onesrow_f = smalls[0:1, 20:21]

    sem_names = ["sx", "sw", "pe", "ve", "ac", "gp", "st", "dn"]

    def program(ctx):
        sy = ctx.eng("sync")
        gp = ctx.eng("gpsimd")
        pe = ctx.eng("tensor")
        ve = ctx.eng("vector")
        ac = ctx.eng("scalar")

        gp.memset(ident[:], 0.0)
        gp.affine_select(out=ident[:], in_=ident[:], compare_op=ALU.not_equal,
                         fill=1.0, base=0, pattern=[[-1, 128]],
                         channel_multiplier=1)
        ins = gp.memset(onesrow_b[:], 1.0)
        ctx.inc("gpsimd", "gp", ins, ev="ident")
        ve.memset(smalls[:], 0.0)
        ve.memset(eps_ap, LN_EPS)
        ve.memset(ebias_ap, EXP_BIAS)
        ve.memset(onesrow_f, 1.0)
        for xv in range(GH * 16):
            ve.memset(vg[:, xv * VT + 128:xv * VT + 129], 1.0)
        ins = ve.memset(onesb[:], 1.0)
        ctx.inc("vector", "ve", ins, ev="init_ve")
        for i in range(2):
            eng, engn = (sy, "sync") if i % 2 == 0 else (ac, "scalar")
            ins = eng.dma_start(out=x0ring[:, (i % 2) * D:(i % 2) * D + D],
                                in_=x_t[i])
            ctx.inc(engn, "sx", ins, ev=f"x{i}", n=16)
        for dcc in range(DC):
            ins = gp.dma_start(out=gatew_s[:, dcc * 8:dcc * 8 + 8],
                               in_=gatew_d[dcc * 128:(dcc + 1) * 128, :])
            ctx.inc("gpsimd", "sw", ins, n=16)
        ctx.ev["gatew"] = ctx.cnt["sw"]

        def qkv_eslice_dma(g, kind, pe_gate=None):
            base_col = (g * 3 + kind) * 512
            slot = (g * 3 + kind) % 2
            if pe_gate is not None:
                ctx.wait("gpsimd", "pe", pe_gate)
            for dcc in range(DC):
                ins = gp.dma_start(
                    out=wring[:, slot * 8192 + dcc * 512:
                              slot * 8192 + dcc * 512 + 512],
                    in_=wqkvt_d[dcc * 128:(dcc + 1) * 128,
                                base_col:base_col + 512])
                ctx.inc("gpsimd", "sw", ins, n=16)
            ctx.ev[f"wqkv_{g}_{kind}"] = ctx.cnt["sw"]

        qkv_eslice_dma(0, 0)
        qkv_eslice_dma(0, 1)

        _LNB = [2, 10, 80, 88]

        def ln_stats(tile_ap, tag, s):
            b = _LNB[s]
            ins = ve.reduce_sum(smalls[:, b:b + 1], tile_ap, axis=AX.X,
                                negate=True)
            ctx.inc("vector", "ve", ins, ev=f"{tag}_sum")
            ctx.wait("vector", "ve", f"{tag}_sum")
            ins = ve.tensor_scalar_mul(smalls[:, b + 1:b + 2],
                                       smalls[:, b:b + 1], 1.0 / D)
            ctx.inc("vector", "ve", ins, ev=f"{tag}_mean")
            ctx.wait("scalar", "ve", f"{tag}_mean")
            ac.activation(out=vartrash[:, 0:D], in_=tile_ap, func=AF.Square,
                          scale=1.0, bias=smalls[:, b + 1:b + 2],
                          accum_out=smalls[:, b + 2:b + 3])
            ac.activation(out=smalls[:, b + 3:b + 4], in_=smalls[:, b + 2:b + 3],
                          func=AF.Ln, scale=1.0 / D, bias=eps_ap)
            ins = ac.activation(out=smalls[:, b + 4:b + 5],
                                in_=smalls[:, b + 3:b + 4],
                                func=AF.Exp, scale=-0.5)
            ctx.inc("scalar", "ac", ins, ev=f"{tag}_rstd")
            return smalls[:, b + 1:b + 2], smalls[:, b + 4:b + 5]

        ln1v = ln1fm[:].rearrange("p (c t) -> p c t", t=S)

        def wtile(g, kind, dcc):
            slot = (g * 3 + kind) % 2
            return wring[:, slot * 8192 + dcc * 512:
                         slot * 8192 + dcc * 512 + 512]

        def emit_q(g, rng=range(8)):
            if rng.start == 0:
                ctx.wait("tensor", "sw", f"wqkv_{g}_0")
            for nacc in rng:
                esub, tt = nacc // 2, nacc % 2
                ps = ps_s0 if nacc % 2 == 0 else ps_s1
                half = ((nacc // 2) % 2) * 512
                pap = ps[:, half:half + 512]
                if nacc >= 4:
                    ctx.wait("tensor", "ac", f"qev_{g}_{nacc-4}")
                elif g > 0:
                    ctx.wait("tensor", "ac", f"a_{g*4-1}_3_exp{2 + nacc % 2}")
                else:
                    ctx.wait("tensor", "ve", f"ln1e_{4*tt+3}_1")
                for dcc in range(DC):
                    ins = pe.matmul(
                        pap, wtile(g, 0, dcc)[:, esub * 128:esub * 128 + 128],
                        ln1v[:, dcc, tt * 512:tt * 512 + 512],
                        start=(dcc == 0), stop=(dcc == DC - 1))
                ctx.inc("tensor", "pe", ins, ev=f"qmm_{g}_{nacc}")
                ctx.wait("scalar", "pe", f"qmm_{g}_{nacc}")
                dst = qg[:, esub * HALF + tt * 512:esub * HALF + tt * 512 + 512]
                ins = ac.activation(out=dst, in_=pap, func=AF.Copy,
                                    scale=INV_SQRT_DH)
                ctx.inc("scalar", "ac", ins, ev=f"qev_{g}_{nacc}")

        def emit_k(g, rng):
            if rng.start == 0:
                ctx.wait("tensor", "sw", f"wqkv_{g}_1")
                qkv_eslice_dma(g, 2, pe_gate=f"qmm_{g}_7")
            for kk in rng:
                tt, esub = kk // 4, kk % 4
                ps = ps_s0 if kk % 2 == 0 else ps_s1
                half = ((kk // 2) % 2) * 512
                pap = ps[:, half:half + 512]
                if g == 0:
                    ctx.wait("tensor", "ve", f"ln1e_{4*tt+3}_1")
                if kk >= 4:
                    ctx.wait("tensor", "ve", f"kev_{g}_{kk-4}")
                else:
                    ctx.wait("tensor", "ac", f"qev_{g}_{4+kk}")
                for dcc in range(DC):
                    ins = pe.matmul(
                        pap, wtile(g, 1, dcc)[:, esub * 128:esub * 128 + 128],
                        ln1v[:, dcc, tt * 512:tt * 512 + 512],
                        start=(dcc == 0), stop=(dcc == DC - 1))
                ctx.inc("tensor", "pe", ins, ev=f"kmm_{g}_{kk}")
                ctx.wait("vector", "pe", f"kmm_{g}_{kk}")
                dst = kg[:, esub * S + tt * 512:esub * S + tt * 512 + 512]
                ins = ve.tensor_copy(dst, pap)
                ctx.inc("vector", "ve", ins, ev=f"kev_{g}_{kk}")

        ln1_mr = {}

        def ln1_stage1(i):
            tile = x0ring[:, (i % 2) * D:(i % 2) * D + D]
            ctx.wait("vector", "sx", f"x{i}")
            if i == 0:
                ctx.wait("vector", "ve", "init_ve")
            ln1_mr[i] = ln_stats(tile, f"ln1_{i}", i % 2)

        def ln1_stage2(i):
            tile = x0ring[:, (i % 2) * D:(i % 2) * D + D]
            lslot = lnscr[:, (i % 2) * D:(i % 2) * D + D]
            m, r = ln1_mr[i]
            ctx.wait("vector", "ac", f"ln1_{i}_rstd")
            if i >= 2:
                ctx.wait("vector", "pe", f"ln1t_{i-2}")
            ins = ve.tensor_scalar(lslot, tile, m, r,
                                   op0=ALU.add, op1=ALU.mult)
            ctx.inc("vector", "ve", ins, ev=f"ln1_{i}_norm")
            if i + 2 < 16:
                eng, engn = (sy, "sync") if i % 2 == 0 else (ac, "scalar")
                ctx.wait(engn, "ve", f"ln1_{i}_norm")
                ins = eng.dma_start(out=x0ring[:, (i % 2) * D:(i % 2) * D + D],
                                    in_=x_t[i + 2])
                ctx.inc(engn, "sx", ins, ev=f"x{i+2}", n=16)
            ctx.wait("tensor", "ve", f"ln1_{i}_norm")
            if i == 0:
                ctx.wait("tensor", "gp", "ident")
            for w in range(2):
                if i > 0 or w > 0:
                    ctx.wait("tensor", "ve",
                             f"ln1e_{i}_{w-1}" if w > 0 else f"ln1e_{i-1}_1")
                for bkk in range(8):
                    dcc = w * 8 + bkk
                    ins = pe.transpose(
                        ps_t[:, bkk * 128:bkk * 128 + 128],
                        lslot.rearrange("p (c f) -> p c f", f=128)[:, dcc],
                        ident[:])
                ctx.inc("tensor", "pe", ins,
                        ev=f"ln1t_{i}" if w == 1 else f"ln1tw_{i}_{w}")
                ctx.wait("vector", "pe",
                         f"ln1t_{i}" if w == 1 else f"ln1tw_{i}_{w}")
                dst = ln1fm[:].rearrange("p (c t) -> p c t", t=S)[
                    :, w * 8:w * 8 + 8, i * 128:(i + 1) * 128]
                ins = ve.tensor_copy(
                    dst, ps_t[:].rearrange("p (c f) -> p c f", f=128))
                ctx.inc("vector", "ve", ins, ev=f"ln1e_{i}_{w}")

        for i in range(17):
            if i < 16:
                ln1_stage1(i)
            if i >= 1:
                ln1_stage2(i - 1)
            if i == 8:
                emit_q(0, range(0, 4))
            if i == 11:
                emit_q(0, range(4, 8))
            if i == 13:
                emit_k(0, range(0, 4))
            if i == 15:
                emit_k(0, range(4, 8))

        if DEBUG_PHASE == 1:
            _dbg(ctx, sy, dbg_d, ln1fm, ("ve", "ln1e_15_1"))
            _epilogue(ctx, sy, gp, pe, ve, ac)
            return

        for g in range(NG):
            if g > 0:
                emit_q(g)
                emit_k(g, range(16))
            else:
                emit_k(0, range(8, 16))

            ctx.wait("tensor", "sw", f"wqkv_{g}_2")
            if g + 1 < NG:
                qkv_eslice_dma(g + 1, 0, pe_gate=f"kmm_{g}_15")
                qkv_eslice_dma(g + 1, 1, pe_gate=f"vmm_{g}_15")
            for tt in range(16):
                ps = ps_s0 if tt % 2 == 0 else ps_s1
                half = ((tt // 2) % 2) * 512
                pap = ps[:, half:half + 512]
                if g == 0:
                    ctx.wait("tensor", "ve", f"ln1e_{tt}_1")
                if tt >= 4:
                    ctx.wait("tensor", "ve", f"vev_{g}_{tt-4}")
                else:
                    ctx.wait("tensor", "ve", f"kev_{g}_{12+tt}")
                for dcc in range(DC):
                    ins = pe.matmul(
                        pap, ln1v[:, dcc, tt * 128:tt * 128 + 128],
                        wtile(g, 2, dcc),
                        start=(dcc == 0), stop=(dcc == DC - 1))
                ctx.inc("tensor", "pe", ins, ev=f"vmm_{g}_{tt}")
                ctx.wait("vector", "pe", f"vmm_{g}_{tt}")
                dst = vg[:].rearrange("p (h t) -> p h t", t=VHS)[
                    :, :, tt * VT:tt * VT + 128]
                ins = ve.tensor_copy(
                    dst, pap.rearrange("p (h t) -> p h t", t=128))
                ctx.inc("vector", "ve", ins, ev=f"vev_{g}_{tt}")

            for h in range(GH):
                head = g * GH + h
                for qc in range(4):
                    tag = f"a_{head}_{qc}"
                    prev = (f"a_{head}_{qc-1}" if qc > 0 else
                            (f"a_{head-1}_3" if h > 0 else None))
                    for wv in range(4):
                        ps = ps_s0 if wv % 2 == 0 else ps_s1
                        if wv >= 2:
                            ctx.wait("tensor", "ac", f"{tag}_exp{wv-2}")
                        elif prev is not None:
                            ctx.wait("tensor", "ac", f"{prev}_exp{wv+2}")
                        else:
                            ctx.wait("tensor", "ve", f"vev_{g}_{14+wv}")
                        for kt4 in range(4):
                            kt = wv * 4 + kt4
                            ins = pe.matmul(
                                ps[:, kt4 * 256:kt4 * 256 + 256],
                                kg[:, h * S + kt * 128:h * S + kt * 128 + 128],
                                qg[:, h * HALF + qc * 256:
                                   h * HALF + qc * 256 + 256],
                                start=True, stop=True)
                        ctx.inc("tensor", "pe", ins, ev=f"{tag}_st{wv}")
                        ctx.wait("scalar", "pe", f"{tag}_st{wv}")
                        if wv == 0:
                            if prev is not None:
                                ctx.wait("scalar", "pe", f"avdone_{head}_{qc-1}"
                                         if qc > 0 else f"avdone_{head-1}_3")
                            elif g > 0:
                                ctx.wait("scalar", "pe", f"avdone_{g*4-1}_3")
                        ins = ac.activation(
                            out=at_b[:, wv * 1024:wv * 1024 + 1024],
                            in_=ps[:], func=AF.Exp, bias=ebias_ap, scale=1.0)
                        ctx.inc("scalar", "ac", ins, ev=f"{tag}_exp{wv}")
                    for qs in range(2):
                        opsum = ps_3 if qs == 0 else ps_d
                        if prev is not None or g > 0:
                            pv = prev if prev is not None else f"a_{g*4-1}_3"
                            ctx.wait("tensor", "ve", f"{pv}_oev{qs}")
                        for kt in range(16):
                            if qs == 0 and kt % 4 == 0:
                                ctx.wait("tensor", "ac", f"{tag}_exp{kt//4}")
                            ins = pe.matmul(
                                opsum[:, 0:129],
                                at_b[:, kt * 256 + qs * 128:
                                     kt * 256 + qs * 128 + 128],
                                vg[:, h * VHS + kt * VT:
                                   h * VHS + kt * VT + 129],
                                start=(kt == 0), stop=(kt == 15))
                        avev = (f"{tag}_av0" if qs == 0
                                else f"avdone_{head}_{qc}")
                        ctx.inc("tensor", "pe", ins, ev=avev)
                        ctx.wait("scalar", "pe", avev)
                        ac.activation(out=smalls[:, 16:17],
                                      in_=opsum[:, 128:129], func=AF.Ln)
                        ins = ac.activation(out=smalls[:, 8 + qs:9 + qs],
                                            in_=smalls[:, 16:17],
                                            func=AF.Exp, scale=-1.0)
                        ctx.inc("scalar", "ac", ins, ev=f"{tag}_rc{qs}")
                        ctx.wait("vector", "ac", f"{tag}_rc{qs}")
                        ctx.wait("vector", "pe", avev)
                        qt = qc * 2 + qs
                        dst = o_all[:, (head * 8 + qt) * 128:
                                    (head * 8 + qt) * 128 + 128]
                        ins = ve.tensor_scalar(dst, opsum[:, 0:128],
                                               smalls[:, 8 + qs:9 + qs],
                                               None, op0=ALU.mult)
                        ctx.inc("vector", "ve", ins, ev=f"{tag}_oev{qs}")

        if DEBUG_PHASE == 3:
            _dbg(ctx, sy, dbg_d, o_all, ("ve", "a_15_3_oev1"))
            _epilogue(ctx, sy, gp, pe, ve, ac)
            return

        def wout_dma(ds, pe_gate):
            slot = ds % 2
            if pe_gate is not None:
                ctx.wait("gpsimd", "pe", pe_gate)
            for ec in range(DC):
                ins = gp.dma_start(
                    out=wring2[:, slot * 8192 + ec * 512:
                               slot * 8192 + ec * 512 + 512],
                    in_=woutt_d[ec * 128:(ec + 1) * 128,
                                ds * 512:(ds + 1) * 512])
                ctx.inc("gpsimd", "sw", ins, n=16)
            ctx.ev[f"wout_{ds}"] = ctx.cnt["sw"]

        wout_dma(0, "vmm_3_15")
        wout_dma(1, None)

        otseq = [(w, head) for w in range(2) for head in range(NH)]
        for i, (w, head) in enumerate(otseq):
            pst = ps_t if i % 2 == 0 else ps_t2
            osem = "ve" if i % 2 == 0 else "ac"
            if i >= 2:
                pw, ph = otseq[i - 2]
                ctx.wait("tensor", osem, f"oev_tw_{pw}_{ph}")
            else:
                ctx.wait("tensor", "ve", "a_15_3_oev1")
            for bkk in range(4):
                qt = w * 4 + bkk
                ins = pe.transpose(
                    pst[:, bkk * 128:bkk * 128 + 128],
                    o_all[:, (head * 8 + qt) * 128:
                          (head * 8 + qt) * 128 + 128],
                    ident[:])
            ctx.inc("tensor", "pe", ins, ev=f"ot_{w}_{head}")
            dst = o_fm[:, head * HALF + w * 512:head * HALF + w * 512 + 512]
            if i % 2 == 0:
                ctx.wait("vector", "pe", f"ot_{w}_{head}")
                ins = ve.tensor_copy(dst, pst[:, 0:512])
                ctx.inc("vector", "ve", ins, ev=f"oev_tw_{w}_{head}")
            else:
                ctx.wait("scalar", "pe", f"ot_{w}_{head}")
                ins = ac.copy(dst, pst[:, 0:512])
                ctx.inc("scalar", "ac", ins, ev=f"oev_tw_{w}_{head}")

        lnA_mr = {}
        ln2_mr = {}
        ln2v = ln2fm[:].rearrange("p (c t) -> p c t", t=HALF)

        def x1_stageA(tt):
            eng, engn = (sy, "sync") if tt % 2 == 0 else (ac, "scalar")
            ctx.wait(engn, "ve", "ln1_15_norm")
            if tt >= 2:
                ctx.wait(engn, "ve", f"x1_{tt-2}_done")
            ins = eng.dma_start(out=x0ring[:, (tt % 2) * D:(tt % 2) * D + D],
                                in_=x_t[tt])
            ctx.inc(engn, "sx", ins, ev=f"x0b_{tt}", n=16)
            tile = attout[:, tt * D:(tt + 1) * D]
            for ds in range(NDS):
                ctx.wait("vector", "ve", f"opev_{ds*8+tt}")
            lnA_mr[tt] = ln_stats(tile, f"lnA_{tt}", tt % 2)

        def x1_stageB(tt):
            tile = attout[:, tt * D:(tt + 1) * D]
            m, r = lnA_mr[tt]
            ctx.wait("vector", "ac", f"lnA_{tt}_rstd")
            ins = ve.tensor_scalar(tile, tile, m, r,
                                   op0=ALU.add, op1=ALU.mult)
            ctx.inc("vector", "ve", ins, ev=f"x1n_{tt}")
            ctx.wait("vector", "sx", f"x0b_{tt}")
            ctx.wait("vector", "ve", f"x1n_{tt}")
            ins = ve.tensor_add(tile, tile,
                                x0ring[:, (tt % 2) * D:(tt % 2) * D + D])
            ctx.inc("vector", "ve", ins, ev=f"x1_{tt}_done")
            ctx.wait("sync", "ve", f"x1_{tt}_done")
            ins = sy.dma_start(out=x1_t[tt], in_=tile)
            ctx.inc("sync", "st", ins, ev=f"x1st_{tt}", n=16)
            ctx.wait("vector", "ve", f"x1_{tt}_done")
            ln2_mr[tt] = ln_stats(tile, f"ln2_{tt}", 2 + tt % 2)

        def x1_stageC(tt):
            tile = attout[:, tt * D:(tt + 1) * D]
            lslot = lnscr[:, (tt % 2) * D:(tt % 2) * D + D]
            m, r = ln2_mr[tt]
            ctx.wait("vector", "ac", f"ln2_{tt}_rstd")
            if tt >= 2:
                ctx.wait("vector", "pe", f"ln2t_{tt-2}")
            ins = ve.tensor_scalar(lslot, tile, m, r,
                                   op0=ALU.add, op1=ALU.mult)
            ctx.inc("vector", "ve", ins, ev=f"ln2_{tt}_norm")
            ctx.wait("tensor", "ve", f"ln2_{tt}_norm")
            for w in range(2):
                if tt > 0 or w > 0:
                    ctx.wait("tensor", "ve",
                             f"ln2e_{tt}_{w-1}" if w > 0 else f"ln2e_{tt-1}_1")
                for bkk in range(8):
                    dcc = w * 8 + bkk
                    ins = pe.transpose(
                        ps_t[:, bkk * 128:bkk * 128 + 128],
                        lslot.rearrange("p (c f) -> p c f", f=128)[:, dcc],
                        ident[:])
                ctx.inc("tensor", "pe", ins,
                        ev=f"ln2t_{tt}" if w == 1 else f"ln2tw_{tt}_{w}")
                ctx.wait("vector", "pe",
                         f"ln2t_{tt}" if w == 1 else f"ln2tw_{tt}_{w}")
                dst = ln2fm[:].rearrange("p (c t) -> p c t", t=HALF)[
                    :, w * 8:w * 8 + 8, tt * 128:(tt + 1) * 128]
                ins = ve.tensor_copy(
                    dst, ps_t[:].rearrange("p (c f) -> p c f", f=128))
                ctx.inc("vector", "ve", ins, ev=f"ln2e_{tt}_{w}")
            if tt == 0:
                ctx.wait("tensor", "sw", "gatew")
            if tt >= 2:
                ctx.wait("tensor", "ve", f"comb_{tt-2}_done")
            ctx.wait("tensor", "ve", f"ln2e_{tt}_1")
            lgp = ps_2 if tt % 2 == 0 else ps_3
            for dcc in range(DC):
                ins = pe.matmul(lgp[:, 0:8],
                                ln2v[:, dcc, tt * 128:(tt + 1) * 128],
                                gatew_s[:, dcc * 8:dcc * 8 + 8],
                                start=(dcc == 0), stop=(dcc == DC - 1))
            ctx.inc("tensor", "pe", ins, ev=f"lg_{tt}")
            pl = lgp[:, 0:8]
            tc = f"cmb_{tt}"
            ctx.wait("vector", "pe", f"lg_{tt}")
            ins = ve.reduce_max(smalls[:, 40:41], pl, axis=AX.X, negate=True)
            ctx.inc("vector", "ve", ins, ev=f"{tc}_m")
            ctx.wait("scalar", "ve", f"{tc}_m")
            ins = ac.activation(out=smalls[:, 48:56], in_=pl, func=AF.Exp,
                                bias=smalls[:, 40:41], scale=1.0,
                                accum_out=smalls[:, 41:42])
            ctx.inc("scalar", "ac", ins, ev=f"{tc}_e")
            ctx.wait("vector", "ac", f"{tc}_e")
            ins = ve.reciprocal(smalls[:, 42:43], smalls[:, 41:42])
            ctx.inc("vector", "ve", ins, ev=f"{tc}_r")
            ctx.wait("vector", "ve", f"{tc}_r")
            ins = ve.tensor_scalar(smalls[:, 48:56], smalls[:, 48:56],
                                   smalls[:, 42:43], None, op0=ALU.mult)
            ctx.inc("vector", "ve", ins, ev=f"{tc}_s")
            ctx.wait("vector", "ve", f"{tc}_s")
            ins = ve.reduce_max(smalls[:, 43:44], smalls[:, 48:56], axis=AX.X)
            ctx.inc("vector", "ve", ins, ev=f"{tc}_m1")
            ctx.wait("vector", "ve", f"{tc}_m1")
            ins = ve.tensor_scalar(smalls[:, 56:64], smalls[:, 48:56],
                                   smalls[:, 43:44], None, op0=ALU.is_equal)
            ctx.inc("vector", "ve", ins, ev=f"{tc}_eq")
            ctx.wait("vector", "ve", f"{tc}_eq")
            ins = ve.scalar_tensor_tensor(smalls[:, 56:64], smalls[:, 56:64],
                                          -1e30, smalls[:, 48:56],
                                          op0=ALU.mult, op1=ALU.add)
            ctx.inc("vector", "ve", ins, ev=f"{tc}_s2")
            ctx.wait("vector", "ve", f"{tc}_s2")
            ins = ve.reduce_max(smalls[:, 44:45], smalls[:, 56:64], axis=AX.X)
            ctx.inc("vector", "ve", ins, ev=f"{tc}_m2")
            ctx.wait("vector", "ve", f"{tc}_m2")
            ins = ve.tensor_scalar(smalls[:, 56:64], smalls[:, 48:56],
                                   smalls[:, 44:45], None, op0=ALU.is_ge)
            ctx.inc("vector", "ve", ins, ev=f"{tc}_ge")
            ctx.wait("vector", "ve", f"{tc}_ge")
            ins = ve.tensor_mul(combtm[:, tt * 8:tt * 8 + 8],
                                smalls[:, 48:56], smalls[:, 56:64])
            ctx.inc("vector", "ve", ins, ev=f"comb_{tt}_done")

        op_sched = [(ds, tt) for pair in (0, 1) for tt in range(8)
                    for ds in (2 * pair, 2 * pair + 1)]
        for gi, (ds, tt) in enumerate(op_sched):
            if gi == 0:
                ctx.wait("tensor", "sw", "wout_0")
                ctx.wait("tensor", "sw", "wout_1")
            if gi == 16:
                ctx.wait("tensor", "sw", "wout_2")
                ctx.wait("tensor", "sw", "wout_3")
            ps = ps_s0 if gi % 2 == 0 else ps_s1
            half = ((gi // 2) % 2) * 512
            pap = ps[:, half:half + 512]
            if gi >= 4:
                pds, ptt = op_sched[gi - 4]
                ctx.wait("tensor", "ve", f"opev_{pds * 8 + ptt}")
            elif gi == 0:
                ctx.wait("tensor", "ac", "oev_tw_0_15")
            if gi == 8:
                ctx.wait("tensor", "ac", "oev_tw_1_15")
            for ec in range(DC):
                ins = pe.matmul(
                    pap,
                    o_fm[:, ec * HALF + tt * 128:ec * HALF + tt * 128 + 128],
                    wring2[:, (ds % 2) * 8192 + ec * 512:
                           (ds % 2) * 8192 + ec * 512 + 512],
                    start=(ec == 0), stop=(ec == DC - 1))
            ctx.inc("tensor", "pe", ins, ev=f"opmm_{ds * 8 + tt}")
            ctx.wait("vector", "pe", f"opmm_{ds * 8 + tt}")
            dst = attout[:, tt * D + ds * 512:tt * D + ds * 512 + 512]
            ins = ve.tensor_copy(dst, pap)
            ctx.inc("vector", "ve", ins, ev=f"opev_{ds * 8 + tt}")
            if gi == 14:
                wout_dma(2, "opmm_7")
            if gi == 15:
                wout_dma(3, "opmm_15")
            if gi >= 17 and (gi - 17) % 2 == 0:
                tdone = (gi - 17) // 2
                x1_stageA(tdone)
                if tdone >= 1:
                    x1_stageB(tdone - 1)
                if tdone >= 2:
                    x1_stageC(tdone - 2)

        if DEBUG_PHASE == 4:
            _dbg(ctx, sy, dbg_d, attout, ("ve", "opev_31"))
            _epilogue(ctx, sy, gp, pe, ve, ac)
            return

        x1_stageB(7)
        x1_stageC(6)
        x1_stageC(7)

        if DEBUG_PHASE == 5:
            _dbg(ctx, sy, dbg_d, ln2fm, ("ve", "ln2e_7_1"))
            _epilogue(ctx, sy, gp, pe, ve, ac)
            return

        if DEBUG_PHASE == 6:
            _dbg(ctx, sy, dbg_d, combtm, ("ve", "comb_7_done"))
            _epilogue(ctx, sy, gp, pe, ve, ac)
            return

        def wgu_dma(e, sw, which, pe_gate):
            src = wg_d if which == 0 else wu_d
            w = SW_HCS[sw]
            if pe_gate is not None:
                ctx.wait("gpsimd", "pe", pe_gate)
            seg = src[e, sw_off[sw]:sw_off[sw + 1]].rearrange(
                "(c p n) -> p c n", p=128, n=w * 128)
            ins = gp.dma_start(
                out=wgring[:, which * 6144:which * 6144 + DC * w * 128]
                .rearrange("p (c n) -> p c n", n=w * 128),
                in_=seg)
            ctx.inc("gpsimd", "sw", ins, n=16, ev=f"wgu_{e}_{sw}_{which}")

        def wdn_dma(e, ds, pe_gate):
            slot = (e * 4 + ds) % 2
            if pe_gate is not None:
                ctx.wait("gpsimd", "pe", pe_gate)
            seg = wdn_d[e, ds].rearrange("(c p) n -> p c n", p=128)
            ins = gp.dma_start(
                out=wdnring[:, slot * NHC * 512:(slot + 1) * NHC * 512]
                .rearrange("p (c n) -> p c n", n=512),
                in_=seg)
            ctx.inc("gpsimd", "sw", ins, n=16, ev=f"wdn_{e}_{ds}")

        wgu_dma(0, 0, 0, "opmm_31")
        wgu_dma(0, 0, 1, None)
        wdn_dma(0, 0, None)
        wdn_dma(0, 1, None)

        def sweep_tile(i, half):
            if i == 0:
                return ps_s0[:, half * 512:half * 512 + 512]
            if i == 1:
                return ps_s1[:, half * 512:half * 512 + 512]
            return (ps_d if half == 0 else ps_d2)[:, 0:512]

        def gslot(e):
            return gbuf[:, 0:NHC * HALF]

        lnF_mr = {}

        def fin_stageA(tt):
            eng, engn = (sy, "sync") if tt % 2 == 0 else (ac, "scalar")
            if tt < 2:
                ctx.wait(engn, "ve", f"x1_{6+tt}_done")
            else:
                ctx.wait(engn, "ve", f"fin_{tt-2}_done")
            ctx.wait(engn, "st", f"x1st_{tt}")
            ins = eng.dma_start(out=x0ring[:, (tt % 2) * D:(tt % 2) * D + D],
                                in_=x1_t[tt])
            ctx.inc(engn, "sx", ins, ev=f"x1rd_{tt}", n=16)
            tile = ffnacc[:, tt * D:(tt + 1) * D]
            for ds in range(NDS):
                ctx.wait("vector", "ve", f"dnev_{NEP-1}_{ds}_{tt}")
            lnF_mr[tt] = ln_stats(tile, f"lnF_{tt}", tt % 2)

        def fin_stageB(tt):
            tile = ffnacc[:, tt * D:(tt + 1) * D]
            m, r = lnF_mr[tt]
            ctx.wait("vector", "ac", f"lnF_{tt}_rstd")
            ins = ve.tensor_scalar(tile, tile, m, r,
                                   op0=ALU.add, op1=ALU.mult)
            ctx.inc("vector", "ve", ins, ev=f"finn_{tt}")
            ctx.wait("vector", "sx", f"x1rd_{tt}")
            ctx.wait("vector", "ve", f"finn_{tt}")
            ins = ve.tensor_add(tile, tile,
                                x0ring[:, (tt % 2) * D:(tt % 2) * D + D])
            ctx.inc("vector", "ve", ins, ev=f"fin_{tt}_done")
            eng, engn = (sy, "sync") if tt % 2 == 0 else (ac, "scalar")
            ctx.wait(engn, "ve", f"fin_{tt}_done")
            ins = eng.dma_start(out=out_t[tt], in_=tile)
            ctx.inc(engn, "st", ins, n=16)

        for e in range(NEP):
            for sw in range(4):
                w = SW_HCS[sw]
                hbase = sum(SW_HCS[:sw])

                def gucol(hcl, half):
                    return gslot(e)[:, (hbase + hcl) * HALF + half * 512:
                                    (hbase + hcl) * HALF + half * 512 + 512]

                ctx.wait("tensor", "sw", f"wgu_{e}_{sw}_0")
                for half in range(2):
                    if e == 0 and sw == 0:
                        ctx.wait("tensor", "ve", f"ln2e_{4*half+3}_1")
                    elif sw == 0:
                        ctx.wait("tensor", "ve", f"guev_{e-1}_3_{half}")
                        if half == 0:
                            ctx.wait("tensor", "ve", f"dnev_{e-1}_3_7")
                    else:
                        ctx.wait("tensor", "ve", f"guev_{e}_{sw-1}_{half}")
                    for dcc in range(DC):
                        for hcl in range(w):
                            ins = pe.matmul(
                                sweep_tile(hcl, half),
                                wgring[:, 0 * 6144 + (dcc * w + hcl) * 128:
                                       0 * 6144 + (dcc * w + hcl) * 128 + 128],
                                ln2v[:, dcc, half * 512:half * 512 + 512],
                                start=(dcc == 0), stop=(dcc == DC - 1))
                    ctx.inc("tensor", "pe", ins, ev=f"gmm_{e}_{sw}_{half}")
                    ctx.wait("scalar", "pe", f"gmm_{e}_{sw}_{half}")
                    for hcl in range(w):
                        ins = ac.activation(out=gucol(hcl, half),
                                            in_=sweep_tile(hcl, half),
                                            func=AF.Silu)
                    ctx.inc("scalar", "ac", ins, ev=f"silu_{e}_{sw}_{half}")
                if sw + 1 < 4:
                    wgu_dma(e, sw + 1, 0, f"gmm_{e}_{sw}_1")
                elif e + 1 < NEP:
                    wgu_dma(e + 1, 0, 0, f"gmm_{e}_{sw}_1")
                ctx.wait("tensor", "sw", f"wgu_{e}_{sw}_1")
                for half in range(2):
                    ctx.wait("tensor", "ac", f"silu_{e}_{sw}_{half}")
                    for dcc in range(DC):
                        for hcl in range(w):
                            ins = pe.matmul(
                                sweep_tile(hcl, half),
                                wgring[:, 6144 + (dcc * w + hcl) * 128:
                                       6144 + (dcc * w + hcl) * 128 + 128],
                                ln2v[:, dcc, half * 512:half * 512 + 512],
                                start=(dcc == 0), stop=(dcc == DC - 1))
                    ctx.inc("tensor", "pe", ins, ev=f"umm_{e}_{sw}_{half}")
                    ctx.wait("vector", "pe", f"umm_{e}_{sw}_{half}")
                    ctx.wait("vector", "ac", f"silu_{e}_{sw}_{half}")
                    for hcl in range(w):
                        ins = ve.tensor_mul(gucol(hcl, half), gucol(hcl, half),
                                            sweep_tile(hcl, half))
                    ctx.inc("vector", "ve", ins, ev=f"guev_{e}_{sw}_{half}")
                if sw + 1 < 4:
                    wgu_dma(e, sw + 1, 1, f"umm_{e}_{sw}_1")
                elif e + 1 < NEP:
                    wgu_dma(e + 1, 0, 1, f"umm_{e}_{sw}_1")

            gdone = f"guev_{e}_3_1"

            if e < NEP - 1:
                sched = [(ds, tt) for ds in range(NDS) for tt in range(8)]
            else:
                sched = ([(0, tt) for tt in range(8)] +
                         [(1, tt) for tt in range(8)] +
                         [(ds, tt) for tt in range(8) for ds in (2, 3)])
            seen_ds = set()
            for gi, (ds, tt) in enumerate(sched):
                if ds not in seen_ds:
                    seen_ds.add(ds)
                    ctx.wait("tensor", "sw", f"wdn_{e}_{ds}")
                if gi == 0:
                    ctx.wait("tensor", "ve", gdone)
                    if e == 0:
                        ctx.wait("vector", "st", "x1st_7")
                dpsum = ps_d if gi % 2 == 0 else ps_d2
                if e > 0 or gi >= 2:
                    if gi >= 2:
                        pds, ptt = sched[gi - 2]
                        pee = e
                    else:
                        pds, ptt = 3, 6 + gi
                        pee = e - 1
                        if e == NEP - 1:
                            pds, ptt = (3, 7) if gi == 1 else (3, 6)
                    ctx.wait("tensor", "ve", f"dnev_{pee}_{pds}_{ptt}")
                for hc in range(NHC):
                    ins = pe.matmul(
                        dpsum[:],
                        gslot(e)[:, hc * HALF + tt * 128:
                                 hc * HALF + tt * 128 + 128],
                        wdnring[:, ((e * 4 + ds) % 2) * NHC * 512 + hc * 512:
                                ((e * 4 + ds) % 2) * NHC * 512 + hc * 512 + 512],
                        start=(hc == 0), stop=(hc == NHC - 1))
                ctx.inc("tensor", "pe", ins, ev=f"dnmm_{e}_{ds}_{tt}")
                ctx.wait("vector", "pe", f"dnmm_{e}_{ds}_{tt}")
                dst = ffnacc[:, tt * D + ds * 512:tt * D + ds * 512 + 512]
                cap = (combtm[:, tt * 8 + e:tt * 8 + e + 1]
                       if e < 8 else None)
                if e == 0:
                    ins = ve.tensor_scalar(dst, dpsum[:], cap, None,
                                           op0=ALU.mult)
                elif e < 8:
                    ins = ve.scalar_tensor_tensor(dst, dpsum[:], cap, dst,
                                                  op0=ALU.mult, op1=ALU.add)
                else:
                    ins = ve.tensor_add(dst, dst, dpsum[:])
                ctx.inc("vector", "ve", ins, ev=f"dnev_{e}_{ds}_{tt}")
                if e < NEP - 1:
                    if tt == 7:
                        nxt_e, nxt_ds = e, ds + 2
                        if nxt_ds >= NDS:
                            nxt_e, nxt_ds = e + 1, nxt_ds - NDS
                        if nxt_e == NEP - 1 or nxt_ds < NDS:
                            if nxt_e < NEP:
                                wdn_dma(nxt_e, nxt_ds, f"dnmm_{e}_{ds}_7")
                else:
                    if gi == 7:
                        wdn_dma(e, 2, f"dnmm_{e}_0_7")
                    if gi == 15:
                        wdn_dma(e, 3, f"dnmm_{e}_1_7")
                    if gi >= 16 and ds == 3:
                        fin_stageA(tt)
                        if tt >= 1:
                            fin_stageB(tt - 1)

        if DEBUG_PHASE == 7:
            for ds in range(NDS):
                for tt in range(8):
                    ctx.wait("sync", "ve", f"dnev_{NEP-1}_{ds}_{tt}")
            _dbg(ctx, sy, dbg_d, ffnacc, ("ve", f"dnev_{NEP-1}_3_7"))
            _epilogue(ctx, sy, gp, pe, ve, ac)
            return

        fin_stageB(7)
        _epilogue(ctx, sy, gp, pe, ve, ac)

    def _dbg(ctx, sy, dbg, buf, wait, rows=128):
        ctx.wait("sync", wait[0], wait[1])
        cols = buf.shape[1]
        n32 = cols if buf.dtype == F32 else cols // 2
        ins = sy.dma_start(out=dbg[0:rows, 0:n32],
                           in_=buf[0:rows, 0:cols].bitcast(F32))
        ctx.inc("sync", "st", ins, n=16)

    def _epilogue(ctx, sy, gp, pe, ve, ac):
        ve_final = ctx.cnt["ve"]
        ins = ve.tensor_copy(smalls[:, 46:47], smalls[:, 20:21])
        ctx.inc("vector", "dn", ins)
        ins = ac.copy(smalls[:, 47:48], smalls[:, 20:21])
        ctx.inc("scalar", "dn", ins)
        ctx.wait("tensor", "ve", ve_final)
        ins = pe.matmul(ps_2[0:1, 400:401], smalls[0:1, 20:21], onesrow_f,
                        start=True, stop=True)
        ctx.inc("tensor", "dn", ins)
        ins = gp.memset(onesb[:, 8:9], 0.0)
        ctx.inc("gpsimd", "dn", ins)
        ctx.wait("sync", "st", ctx.cnt["st"])
        ctx.wait("sync", "dn", ctx.cnt["dn"])
        ctx.wait("sync", "sx", ctx.cnt["sx"])
        ctx.wait("sync", "sw", ctx.cnt["sw"])
        if ctx.cur == "sync":
            for s_ in sem_names:
                ctx.nc.sync.sem_clear(sems[s_])

    from contextlib import ExitStack
    es = ExitStack()
    sems = {}
    for s_ in sem_names:
        sems[s_] = es.enter_context(nc.semaphore(s_))

    ctx = Ctx(nc, sems)
    ctx.begin(None)
    program(ctx)

    block_cm = nc.Block()
    block = block_cm.__enter__()

    def mk(engname):
        def body(_eng):
            ctx.begin(engname)
            for s_ in ctx.waits.get(engname, []):
                getattr(nc, engname).sem_clear(sems[s_])
            program(ctx)
        return body

    block.sync(mk("sync"))
    block.gpsimd(mk("gpsimd"))
    block.vector(mk("vector"))
    block.scalar(mk("scalar"))
    block.tensor(mk("tensor"))
    block_cm.__exit__(None, None, None)
    es.close()
    return nc



def _pack_weights(Wqkv, Wout, gate_w, w_gate, w_up, w_down, sw_gate, sw_up,
                  sw_down):
    WqkvT = np.ascontiguousarray(np.asarray(Wqkv, np.float32).T)
    cols = []
    for g in range(NG):
        cols += list(range(g * 512, g * 512 + 512))
        cols += list(range(2048 + g * 512, 2048 + g * 512 + 512))
        cols += list(range(4096 + g * 512, 4096 + g * 512 + 512))
    wqkvt = np.ascontiguousarray(WqkvT[:, cols])
    woutt = np.ascontiguousarray(np.asarray(Wout, np.float32).T)
    gatew = np.ascontiguousarray(np.asarray(gate_w, np.float32))
    swg = np.asarray(sw_gate, np.float32).reshape(D, 2, HFF).transpose(1, 0, 2)
    swu = np.asarray(sw_up, np.float32).reshape(D, 2, HFF).transpose(1, 0, 2)
    wg10 = np.concatenate([np.asarray(w_gate, np.float32), swg], axis=0)
    wu10 = np.concatenate([np.asarray(w_up, np.float32), swu], axis=0)

    def sweep_pack(m10):
        out = np.empty((NEP, D * HFF), np.float32)
        for e in range(NEP):
            m = m10[e]
            parts = []
            hb = 0
            for w in SW_HCS:
                seg = m[:, hb * 128:(hb + w) * 128]
                seg = np.ascontiguousarray(seg.reshape(DC, 128, w * 128))
                parts.append(seg.reshape(-1))
                hb += w
            out[e] = np.concatenate(parts)
        return np.ascontiguousarray(out)

    wg = sweep_pack(wg10)
    wu = sweep_pack(wu10)

    swd = np.asarray(sw_down, np.float32).reshape(2, HFF, D)
    wdn10 = np.concatenate([np.asarray(w_down, np.float32), swd], axis=0)
    wdn = np.ascontiguousarray(
        wdn10.reshape(NEP, HFF, NDS, 512).transpose(0, 2, 1, 3))
    return wqkvt, woutt, gatew, wg, wu, wdn


def kernel(L0_attn_x, L0_attn_Wqkv, L0_attn_Wout, L0_ffn_gate_w, L0_ffn_w_gate,
           L0_ffn_w_up, L0_ffn_w_down, L0_ffn_sw_gate, L0_ffn_sw_up,
           L0_ffn_sw_down):
    X = np.asarray(L0_attn_x, np.float32)
    wqkvt, woutt, gatew, wg, wu, wdn = _pack_weights(
        L0_attn_Wqkv, L0_attn_Wout, L0_ffn_gate_w, L0_ffn_w_gate, L0_ffn_w_up,
        L0_ffn_w_down, L0_ffn_sw_gate, L0_ffn_sw_up, L0_ffn_sw_down)

    nc = build_program()
    in_maps = []
    for c in range(8):
        b, j = c // 2, c % 2
        xb = X[b] if j == 0 else np.ascontiguousarray(
            np.roll(X[b], -HALF, axis=0))
        in_maps.append({"x": xb, "wqkvt": wqkvt, "woutt": woutt,
                        "gatew": gatew, "wg": wg, "wu": wu, "wdn": wdn})
    res = run_bass_kernel_spmd(nc, in_maps, core_ids=list(range(8)))
    out = np.empty((NB, S, D), np.float32)
    for c in range(8):
        b, j = c // 2, c % 2
        out[b, j * HALF:(j + 1) * HALF] = res.results[c]["out"]
    return out
